# version 09
# baseline (speedup 1.0000x reference)
import contextlib
import math

import numpy as np
import ml_dtypes

import concourse.bass as bass
import concourse.mybir as mybir
from concourse.bass_utils import run_bass_kernel_spmd
from concourse.aot_env import allow_ici_all_to_all

F32 = mybir.dt.float32
BF16 = mybir.dt.bfloat16
I32 = mybir.dt.int32
AL = mybir.AluOpType
AF = mybir.ActivationFunctionType
AX = mybir.AxisListType

NEG = -1.0e30


def make_cfg(full=True):
    if full:
        return dict(S=2048, D=4096, NH=16, NKV=4, R=16, E=8, NE=15, NCORES=8)
    return dict(S=1024, D=512, NH=4, NKV=1, R=16, E=8, NE=15, NCORES=2)


class Task:
    __slots__ = ("eng", "fn", "deps", "inc", "needs", "value", "name", "sem_key",
                 "barwait")

    def __init__(self, eng, fn, deps, inc, name, sem_key):
        self.eng = eng
        self.fn = fn
        self.deps = deps
        self.inc = inc
        self.needs = inc == 16
        self.value = None
        self.name = name
        self.sem_key = sem_key
        self.barwait = None


class Buf:
    __slots__ = ("w", "rs")

    def __init__(self):
        self.w = None
        self.rs = []

    def wdeps(self):
        d = list(self.rs)
        if self.w is not None:
            d.append(self.w)
        return d

    def setw(self, t):
        self.w = t
        self.rs = []

    def rdep(self):
        return [self.w] if self.w is not None else []

    def rd(self, t):
        self.rs.append(t)


class Sched:
    ENGINES = ("pe", "dve", "act", "pool", "sync")

    def __init__(self):
        self.tasks = []
        self.last = {}
        self.lastsem = {}
        self.nbar = 0

    def add(self, eng, fn, deps=(), dma=False, name="", sem_key=None):
        ds = [d for d in deps if d is not None]
        t = Task(eng, fn, ds, 16 if dma else 1, name, sem_key or eng)
        for d in ds:
            if not (d.eng == eng == "pe" and d.inc == 1):
                d.needs = True
        self.tasks.append(t)
        self.last[eng] = t
        self.lastsem[t.sem_key] = t
        return t

    def barrier(self, name="bar"):
        self.nbar += 1
        engines = [e for e in self.ENGINES if e in self.last]
        target = self.nbar * len(engines)
        for e in engines:
            deps = [t for t in self.lastsem.values()
                    if t.eng == e and t.inc == 16]
            t = self.add(e, lambda eng: eng.nop(), deps=deps, name=name,
                         sem_key="bar")
            t.needs = True
        for e in engines:
            w = Task(e, lambda eng: None, [], 1, name + "w", e)
            w.barwait = target
            self.tasks.append(w)
            self.last[e] = w

    def finalize(self):
        cum = {}
        for t in self.tasks:
            if t.barwait is not None:
                continue
            if t.eng != "pe":
                t.needs = True
            if t.needs:
                cum[t.sem_key] = cum.get(t.sem_key, 0) + t.inc
                t.value = cum[t.sem_key]
        return sorted(cum.keys())

    def replay(self, eng_key, eng, sems):
        wm = {}
        for t in self.tasks:
            if t.eng != eng_key:
                continue
            bw = getattr(t, "barwait", None)
            if bw is not None:
                eng.wait_ge(sems["bar"], bw)
                continue
            for d in t.deps:
                if d.eng == eng_key == "pe" and d.inc == 1:
                    continue
                if d.inc == 16 and t.inc == 16 and d.sem_key == t.sem_key:
                    continue
                if d.value is None:
                    continue
                if wm.get(d.sem_key, 0) < d.value:
                    eng.wait_ge(sems[d.sem_key], d.value)
                    wm[d.sem_key] = d.value
            inst = t.fn(eng)
            if t.needs:
                assert inst is not None, f"task {t.name} needs inc but emitted nothing"
                inst.then_inc(sems[t.sem_key], t.inc)


class SAlloc:
    def __init__(self, nc):
        self.nc = nc
        self.base = ((nc.sbuf_base + 63) // 64) * 64
        self.top = nc.sbuf_top
        self.cur = self.base

    def take(self, name, shape, dtype, align=64):
        self.cur = ((self.cur + align - 1) // align) * align
        per = int(np.prod(shape[1:])) * mybir.dt.size(dtype)
        assert self.cur + per <= self.top, (
            f"SBUF overflow at {name}: {self.cur}+{per} > {self.top}"
        )
        t = self.nc.alloc_sbuf_tensor_at(name, list(shape), dtype, offset=self.cur)
        self.cur += per
        return t

    def mark(self):
        return self.cur

    def reset(self, m):
        self.cur = m


def build_nc(cfg):
    S, D, NH, NKV = cfg["S"], cfg["D"], cfg["NH"], cfg["NKV"]
    R, E, NE = cfg["R"], cfg["E"], cfg["NE"]
    NCORES = cfg["NCORES"]
    SM = S // 2
    DC = D // 128
    MYQ = NH * 128
    MYKV = NKV * 128
    DATT = 2 * NH * 128
    DCA = DATT // 128
    QT = S // 128
    TSUB = SM // 128
    NREP = 4
    ER = E * R
    ISQ = 1.0 / math.sqrt(128.0)
    OGW = 256
    OG = D // OGW
    EPS = 1e-5
    NTT = SM // 512
    CW = min(2048, D)
    assert SM % 512 == 0 and TSUB % 4 == 0 and DC % 4 == 0 and ER == 128
    assert DCA % 4 == 0 and NH * S == DCA * SM

    nc = bass.Bass("TRN2", target_bir_lowering=False)

    x_b = nc.dram_tensor("x_b", [S, D], F32, kind="ExternalInput")
    x_mine = nc.dram_tensor("x_mine", [SM, D], F32, kind="ExternalInput")
    wq_de = nc.dram_tensor("wq_de", [D, MYQ], F32, kind="ExternalInput")
    wk_de = nc.dram_tensor("wk_de", [D, MYKV], F32, kind="ExternalInput")
    wv_sl = nc.dram_tensor("wv_sl", [D, MYKV], F32, kind="ExternalInput")
    wo = nc.dram_tensor("wo", [DATT, D], F32, kind="ExternalInput")
    bWT = nc.dram_tensor("bWT", [D, D], F32, kind="ExternalInput")
    bb = nc.dram_tensor("bb", [1, D], F32, kind="ExternalInput")
    rWT = nc.dram_tensor("rWT", [D, NE], F32, kind="ExternalInput")
    rb = nc.dram_tensor("rb", [1, NE], F32, kind="ExternalInput")
    A_T = nc.dram_tensor("A_T", [D, ER], F32, kind="ExternalInput")
    B_r = nc.dram_tensor("B_r", [ER, D], F32, kind="ExternalInput")
    em = nc.dram_tensor("em", [NE, 1], I32, kind="ExternalInput")
    c_ident = nc.dram_tensor("c_ident", [128, 128], BF16, kind="ExternalInput")
    c_tri = nc.dram_tensor("c_tri", [128, 128], F32, kind="ExternalInput")
    c_cos2 = nc.dram_tensor("c_cos2", [128, S], F32, kind="ExternalInput")
    c_sinf = nc.dram_tensor("c_sinf", [128, S], F32, kind="ExternalInput")
    c_swap = nc.dram_tensor("c_swap", [128, 128], BF16, kind="ExternalInput")
    c_eidx = nc.dram_tensor("c_eidx", [NE, 128], F32, kind="ExternalInput")
    c_stair = nc.dram_tensor("c_stair", [128, 896], BF16, kind="ExternalInput")
    c_onec = nc.dram_tensor("c_onec", [128, 1], BF16, kind="ExternalInput")
    c_ones = nc.dram_tensor("c_ones", [1, 512], BF16, kind="ExternalInput")
    c_f32s = nc.dram_tensor("c_f32s", [128, 16], F32, kind="ExternalInput")
    c_soff = nc.dram_tensor("c_soff", [1, 1], I32, kind="ExternalInput")
    y = nc.dram_tensor("y", [SM, D], F32, kind="ExternalOutput")

    s_loc = nc.dram_tensor("s_loc", [MYQ, S], BF16)
    NCCH = 4
    s_g = [nc.dram_tensor(f"s_g{k}", [2 * (MYQ // NCCH), S], BF16)
           for k in range(NCCH)]
    bb_bf_d = nc.dram_tensor("bb_bf_d", [1, D], BF16)

    PSA = nc.alloc_psum_tensor("psA", [128, 2048], F32)
    AV0 = nc.alloc_psum_tensor("psAV0", [128, 512], F32)
    AV1 = nc.alloc_psum_tensor("psAV1", [128, 512], F32)
    PT0 = nc.alloc_psum_tensor("psPT0", [128, 512], BF16)
    PT1 = nc.alloc_psum_tensor("psPT1", [128, 512], BF16)
    PTb = [PT0, PT1]
    AVb = [AV0, AV1]

    sa = SAlloc(nc)
    sb_ident = sa.take("ident", [128, 128], BF16)
    sb_tri = sa.take("tri", [128, 128], F32)
    sb_f32s = sa.take("f32s", [128, 16], F32)
    sb_soff = sa.take("soff", [1, 1], I32)
    EXPC = 12.0
    ACT_CONSTS = [0.0, 1.0, EPS, ISQ, 2.0, -EXPC]
    for _i, _v in enumerate(ACT_CONSTS):
        nc.const_aps.register(F32, float(_v), sb_f32s[:, _i:_i + 1])
    sb_ones = sa.take("ones", [1, 512], BF16)
    sb_eidx = sa.take("eidx", [NE, 128], F32)
    sb_em = sa.take("em", [NE, 1], I32)
    sb_emf = sa.take("emf", [NE, 1], F32)
    sb_mrep = sa.take("mrep", [NE, 128], BF16)
    sb_swap = sa.take("swap", [128, 128], BF16)
    sb_stair = sa.take("stair", [128, 896], BF16)
    sb_onec = sa.take("onec", [128, 1], BF16)
    sb_Q = sa.take("Qsb", [128, NH * S], BF16)
    sb_K = sa.take("Ksb", [128, NKV * S], BF16)
    sb_V = sa.take("Vsb", [128, (S // 128) * MYKV], BF16)
    sb_xn = sa.take("xnf", [128, DC * SM], BF16)
    sb_sall = nc.alloc_sbuf_tensor_at(
        "sall", [128, DCA * SM], BF16, offset=nc.lookup_mloc(sb_Q).addr)
    if NKV * S * 2 >= DCA * OGW * 2 and (S // 128) * MYKV * 2 >= DC * OGW * 2:
        sb_wosl = nc.alloc_sbuf_tensor_at(
            "wosl", [128, DCA * OGW], BF16, offset=nc.lookup_mloc(sb_K).addr)
        sb_bwsl = nc.alloc_sbuf_tensor_at(
            "bwsl", [128, DC * OGW], BF16, offset=nc.lookup_mloc(sb_V).addr)
    else:
        sb_wosl = sa.take("wosl", [128, DCA * OGW], BF16)
        sb_bwsl = sa.take("bwsl", [128, DC * OGW], BF16)

    W0 = sa.mark()
    sb_cosh = sa.take("cosh", [128, SM], F32)
    sb_sinh = sa.take("sinh", [128, SM], F32)
    sb_xst = sa.take("xst", [128, D], F32)
    sb_xtm = sa.take("xtm", [128, 1024], BF16)
    sb_bn = sa.take("bn", [128, 6 * (D // 512 if D >= 512 else 1)], F32)
    sb_mv = sa.take("mv", [128, 2], F32)
    sb_sd = sa.take("sd", [128, 1], F32)
    sb_rstd = sa.take("rstd", [128, 1], F32)
    sb_negmr = sa.take("negmr", [128, 1], F32)
    sb_wst = [sa.take(f"wst{i}", [128, 512], F32) for i in range(3)]
    sb_wbf = [sa.take(f"wbf{i}", [128, 512], BF16) for i in range(3)]
    sb_qb = sa.take("qb", [128, 512], BF16)
    sb_ub = sa.take("ub", [128, 512], BF16)
    sb_rt = [sa.take(f"rt{i}", [128, 512], F32) for i in range(2)]
    p1_end = sa.mark()

    sa.reset(W0)
    sb_Brbf = sa.take("Brbf", [128, D], BF16)
    sb_hw = sa.take("hw", [128, SM], BF16)
    W1 = sa.mark()
    sb_rbbf = sa.take("rbbf", [1, 64], BF16)
    sb_bb16 = sa.take("bb16", [1, 1024], BF16)
    sb_rwbf = sa.take("rwbf", [128, NE * DC], BF16)
    sb_abf = [sa.take(f"abf{i}", [128, ER], BF16) for i in range(2)]
    sb_sst = sa.take("sst", [128, CW], F32)
    sb_bst = sa.take("bst", [1, 1024], F32)
    sb_lg = sa.take("lg", [128, NE], F32)
    sb_pm = sa.take("pm", [128, 1], F32)
    sb_ls = sa.take("ls", [128, 1], F32)
    sb_rr = sa.take("rr3", [128, 1], F32)
    sb_pr = sa.take("pr", [128, NE], F32)
    sb_t8 = sa.take("t8", [128, 8], F32)
    sb_msk = sa.take("msk", [128, NE], F32)
    sb_mpb = sa.take("mpb", [128, NE], BF16)
    sb_mT = sa.take("mT", [NE, SM], BF16)
    sb_wsb = sa.take("wsb", [128, SM], F32)
    p3b_end = sa.mark()

    sa.reset(W1)
    sb_E = sa.take("Esb", [128, QT * 512], BF16)
    sb_lrow = sa.take("lrow", [1, 512], F32)
    sb_cf = sa.take("cfrow", [1, 512], F32)
    sb_cb = sa.take("cbrow", [1, 512], BF16)
    sb_bc = [sa.take(f"bc{i}", [128, 512], F32) for i in range(2)]
    sb_sf = [sa.take(f"sf{i}", [128, 512], BF16) for i in range(2)]
    p2_end = sa.mark()

    sa.reset(W1)
    sb_bbs = sa.take("bbs", [1, 512], BF16)
    sb_wf = [sa.take(f"wf{i}", [128, 2048], F32) for i in range(2)]
    sb_g = [sa.take(f"g{i}", [128, OGW], F32) for i in range(2)]
    sb_sg = [sa.take(f"sg{i}", [128, OGW], F32) for i in range(2)]
    sb_xr = [sa.take(f"xr{i}", [128, OGW], F32) for i in range(2)]
    sb_out = [sa.take(f"out{i}", [128, OGW], F32) for i in range(2)]
    p3c_end = sa.mark()

    Sc = Sched()
    bufs = {}

    def B(name):
        if name not in bufs:
            bufs[name] = Buf()
        return bufs[name]

    ARN = [B(f"arn{k}") for k in range(4)]
    PTB = [B("ptb0"), B("ptb1")]
    AVB = [B("avb0"), B("avb1")]
    STOP = cfg.get("stop")

    def _emit():
        for _t in Sc.tasks:
            if _t.eng == "act":
                _t.deps.append(t_const)
        sem_keys = Sc.finalize()
        stack = contextlib.ExitStack()
        sems = {k: stack.enter_context(nc.semaphore(f"sem_{k}")) for k in sem_keys}
        with stack:
            with nc.Block() as block:
                @block.tensor
                def _(eng):
                    Sc.replay("pe", eng, sems)

                @block.vector
                def _(eng):
                    Sc.replay("dve", eng, sems)

                @block.scalar
                def _(eng):
                    Sc.replay("act", eng, sems)

                @block.gpsimd
                def _(eng):
                    with allow_ici_all_to_all():
                        Sc.replay("pool", eng, sems)

                @block.sync
                def _(eng):
                    Sc.replay("sync", eng, sems)
        return nc

    def dma(src_ap, dst_ap, deps=(), name="dma", q="sync", sem_key=None):
        return Sc.add(q, lambda e, o=dst_ap, i=src_ap: e.dma_start(out=o, in_=i),
                      deps=deps, dma=True, name=name, sem_key=sem_key)

    def act_copy(eng_dst_src):
        pass

    t_c = None
    for src, dst in ((c_f32s[:, :], sb_f32s[:, :]),
                     (c_ident[:, :], sb_ident[:, :]), (c_tri[:, :], sb_tri[:, :]),
                     (c_ones[:, :], sb_ones[:, :]), (c_eidx[:, :], sb_eidx[:, :]),
                     (em[:, :], sb_em[:, :]), (c_soff[:, :], sb_soff[:, :]),
                     (c_swap[:, :], sb_swap[:, :]), (c_stair[:, :], sb_stair[:, :]),
                     (c_onec[:, :], sb_onec[:, :])):
        t_c = dma(src, dst, name="const", sem_key="ldc")
    t_const = t_c

    if STOP == "c0":
        Sc.barrier("fin")
        return _emit()

    def xn_col(dc, t0):
        return sb_xn[:, dc * SM + t0: dc * SM + t0 + 512]

    def emit_ln_transpose(xsrc, row0, dstcol):
        bx, bxt = B("xst"), B("xtm")
        t_ld = dma(xsrc[row0:row0 + 128, :], sb_xst[:, :], deps=bx.wdeps(),
                   name="xld", sem_key="ldx")
        bx.setw(t_ld)
        nbn = max(1, D // 512)
        t_bn = None
        wdep = B("bn").wdeps()
        for i in range(nbn):
            w = min(512, D)
            t_bn = Sc.add("dve", lambda e, i=i, w=w: e.bn_stats(
                sb_bn[:, i * 6:(i + 1) * 6], sb_xst[:, i * 512:i * 512 + w]),
                deps=[t_ld] + wdep, name="bnst")
        bx.rd(t_bn)
        B("bn").setw(t_bn)
        if STOP and STOP.startswith("ln_ld"):
            return [t_bn]
        t_ag = Sc.add("dve", lambda e: e.bn_aggr(sb_mv[:, :], sb_bn[:, 0:nbn * 6]),
                      deps=[t_bn] + B("mv").wdeps(), name="bnag")
        B("mv").setw(t_ag)
        B("bn").rd(t_ag)
        if STOP and STOP.startswith("ln_bn"):
            return [t_ag]
        t_sd = Sc.add("act", lambda e: e.activation(
            sb_sd[:, :], sb_mv[:, 1:2], AF.Sqrt, bias=EPS, scale=1.0),
            deps=[t_ag] + B("sdv").wdeps(), name="sd")
        B("sdv").setw(t_sd)
        B("mv").rd(t_sd)
        t_rs = Sc.add("dve", lambda e: e.reciprocal(sb_rstd[:, :], sb_sd[:, :]),
                      deps=[t_sd] + B("rstdv").wdeps(), name="rstd")
        B("rstdv").setw(t_rs)
        B("sdv").rd(t_rs)
        t_nm = Sc.add("dve", lambda e: e.scalar_tensor_tensor(
            sb_negmr[:, :], sb_mv[:, 0:1], -1.0, sb_rstd[:, :], AL.mult, AL.mult),
            deps=[t_rs] + B("negv").wdeps(), name="negmr")
        B("negv").setw(t_nm)
        B("mv").rd(t_nm)
        B("rstdv").rd(t_nm)
        if STOP and STOP.startswith("ln_sd"):
            return [t_nm]
        evs = []
        nhalf = max(1, D // 1024)
        hw_ = min(1024, D)
        for hf in range(nhalf):
            t_xn = Sc.add("act", lambda e, hf=hf, hw_=hw_: e.activation(
                sb_xtm[:, 0:hw_], sb_xst[:, hf * hw_:(hf + 1) * hw_], AF.Identity,
                bias=sb_negmr[:, :], scale=sb_rstd[:, :]),
                deps=[t_nm, t_ld] + bxt.wdeps(), name="xnorm")
            bxt.setw(t_xn)
            bx.rd(t_xn)
            B("rstdv").rd(t_xn)
            B("negv").rd(t_xn)
            if STOP and STOP.startswith("ln_no"):
                evs.append(t_xn)
                continue
            ndg = hw_ // 512
            for dg in range(ndg):
                pb = dg % 2
                tts = []
                for i in range(4):
                    dcl = dg * 4 + i
                    dc = hf * 16 + dcl
                    tts.append(Sc.add("pe", lambda e, dcl=dcl, pb=pb, i=i: e.transpose(
                        PTb[pb][:, i * 128:(i + 1) * 128],
                        sb_xtm[:, dcl * 128:(dcl + 1) * 128], sb_ident[:, :]),
                        deps=[t_xn, t_const] + (PTB[pb].wdeps() if i == 0 else []),
                        name="xT"))
                PTB[pb].setw(tts[-1])
                bxt.rd(tts[-1])
                ap = sb_xn[:, :].rearrange("p (d t) -> p d t", t=SM)
                dc0 = (hf * hw_) // 128 + dg * 4
                dst = ap[:, dc0:dc0 + 4, dstcol:dstcol + 128]
                t_ev = Sc.add("act", lambda e, pb=pb, dst=dst: e.activation(
                    dst, PTb[pb][:, :], AF.Copy, bias=0.0, scale=1.0),
                    deps=[tts[-1]] + B("xnf").wdeps(), name="xTe")
                PTB[pb].rd(t_ev)
                evs.append(t_ev)
        return evs

    def stream_chunk(hbm_ap, w, st, name):
        bst, bcv = B(f"wstb{st}"), B(f"wbfb{st}")
        t_ld = dma(hbm_ap, sb_wst[st][:, 0:w], deps=bst.wdeps(), name=name,
                   sem_key=f"ldw{st}")
        bst.setw(t_ld)
        if st == 2:
            t_cv = Sc.add("act", lambda e, st=st, w=w: e.activation(
                sb_wbf[st][:, 0:w], sb_wst[st][:, 0:w], AF.Copy,
                bias=0.0, scale=1.0), deps=[t_ld] + bcv.wdeps(), name=name + "c")
        else:
            eng = "pool" if st == 0 else "dve"
            t_cv = Sc.add(eng, lambda e, st=st, w=w: e.tensor_copy(
                sb_wbf[st][:, 0:w], sb_wst[st][:, 0:w]), deps=[t_ld] + bcv.wdeps(),
                name=name + "c")
        bcv.setw(t_cv)
        bst.rd(t_cv)
        return t_cv

    def rope_evac(mm_stop, slck, col, th, tt, dst_sb, t_tab):
        bq, bu = B("qbb"), B("ubb")
        t_cp = Sc.add("act", lambda e, slck=slck: e.activation(
            sb_qb[:, :], PSA[:, slck * 512:(slck + 1) * 512], AF.Copy,
            bias=0.0, scale=1.0), deps=[mm_stop] + bq.wdeps(), name="rcp")
        bq.setw(t_cp)
        ARN[slck].rd(t_cp)
        pb = slck % 2
        t_sw = Sc.add("pe", lambda e, pb=pb: e.matmul(
            AVb[pb][:, :], sb_swap[:, :], sb_qb[:, :], start=True, stop=True),
            deps=[t_cp, t_const] + AVB[pb].wdeps(), name="rsw")
        AVB[pb].setw(t_sw)
        bq.rd(t_sw)
        t_ue = Sc.add("act", lambda e, pb=pb: e.activation(
            sb_ub[:, :], AVb[pb][:, :], AF.Copy, bias=0.0, scale=1.0),
            deps=[t_sw] + bu.wdeps(), name="rue")
        bu.setw(t_ue)
        AVB[pb].rd(t_ue)
        c0 = tt * 512
        csl = sb_cosh[:, c0:c0 + 512]
        ssl = sb_sinh[:, c0:c0 + 512]
        t1 = Sc.add("dve", lambda e, csl=csl: e.tensor_tensor(
            sb_rt[0][:, :], sb_qb[:, :], csl, AL.mult),
            deps=[t_cp, t_tab] + B("rtb0").wdeps(), name="rpA")
        B("rtb0").setw(t1)
        bq.rd(t1)
        t2 = Sc.add("pool", lambda e, ssl=ssl: e.tensor_tensor(
            sb_rt[1][:, :], sb_ub[:, :], ssl, AL.mult),
            deps=[t_ue, t_tab] + B("rtb1").wdeps(), name="rpB")
        B("rtb1").setw(t2)
        bu.rd(t2)
        tro = Sc.add("dve", lambda e, col=col: e.tensor_tensor(
            dst_sb[:, col:col + 512], sb_rt[0][:, :], sb_rt[1][:, :], AL.add),
            deps=[t1, t2], name="rpR")
        B("rtb0").rd(tro)
        B("rtb1").rd(tro)
        B("cosb").rd(tro)
        return tro, tro

    q_fin, k_fin, v_fin = [], [], []
    for th in range(2):
        t_cos = dma(c_cos2[:, th * SM:(th + 1) * SM], sb_cosh[:, :],
                    deps=B("cosb").wdeps(), name="cosld", sem_key="ldcos")
        t_sin = dma(c_sinf[:, th * SM:(th + 1) * SM], sb_sinh[:, :],
                    deps=B("cosb").wdeps(), name="sinld", sem_key="ldcos")
        B("cosb").setw(t_sin)
        if STOP == "ln_cos":
            break
        xn_evs = []
        for ts in range(TSUB):
            xn_evs += emit_ln_transpose(x_b, th * SM + ts * 128, ts * 128)
        B("xnf").setw(xn_evs[-1])
        xn_rdy = xn_evs[-1]
        if STOP and STOP.startswith("ln"):
            break

        for ocg in range(NH // 2):
            oc0 = ocg * 2
            mm_stop = {}
            for dc in range(DC):
                st = dc % 3
                t_cv = stream_chunk(
                    wq_de[dc * 128:(dc + 1) * 128, oc0 * 128:(oc0 + 2) * 128],
                    256, st, "wq")
                for tt in range(NTT):
                    for oi in range(2):
                        k = oi * NTT + tt
                        deps = [t_cv, xn_rdy]
                        if dc == 0:
                            deps += ARN[k].wdeps()
                        t_mm = Sc.add("pe", lambda e, k=k, st=st, oi=oi, tt=tt, dc=dc:
                                      e.matmul(
                            PSA[:, k * 512:(k + 1) * 512],
                            sb_wbf[st][:, oi * 128:(oi + 1) * 128],
                            xn_col(dc, tt * 512),
                            start=(dc == 0), stop=(dc == DC - 1)),
                            deps=deps, name="qmm")
                        if dc == 0:
                            ARN[k].setw(t_mm)
                        B(f"wbfb{st}").rd(t_mm)
                        if dc == DC - 1:
                            mm_stop[k] = t_mm
                            B("xnf").rd(t_mm)
            for oi in range(2):
                for tt in range(NTT):
                    k = oi * NTT + tt
                    col = (oc0 + oi) * S + th * SM + tt * 512
                    tro, tri_ = rope_evac(mm_stop[k], k, col, th, tt, sb_Q, t_sin)
                    q_fin = [tro, tri_]

        if STOP == "q":
            break
        for tt in range(NTT):
            mm_stop = {}
            for dc in range(DC):
                st = dc % 3
                t_cv = stream_chunk(wk_de[dc * 128:(dc + 1) * 128, :], MYKV, st, "wk")
                for oc in range(NKV):
                    deps = [t_cv, xn_rdy]
                    if dc == 0:
                        deps += ARN[oc].wdeps()
                    t_mm = Sc.add("pe", lambda e, oc=oc, st=st, tt=tt, dc=dc: e.matmul(
                        PSA[:, oc * 512:(oc + 1) * 512],
                        sb_wbf[st][:, oc * 128:(oc + 1) * 128],
                        xn_col(dc, tt * 512),
                        start=(dc == 0), stop=(dc == DC - 1)),
                        deps=deps, name="kmm")
                    if dc == 0:
                        ARN[oc].setw(t_mm)
                    B(f"wbfb{st}").rd(t_mm)
                    if dc == DC - 1:
                        mm_stop[oc] = t_mm
                        B("xnf").rd(t_mm)
            for oc in range(NKV):
                col = oc * S + th * SM + tt * 512
                tro, tri_ = rope_evac(mm_stop[oc], oc, col, th, tt, sb_K, t_sin)
                k_fin = [tro, tri_]

        if STOP == "k":
            break
        for tqg in range(TSUB // 4):
            mm_stop = {}
            for dc in range(DC):
                st = dc % 3
                t_cv = stream_chunk(wv_sl[dc * 128:(dc + 1) * 128, :], MYKV, st, "wv")
                for tq in range(4):
                    deps = [t_cv, xn_rdy]
                    if dc == 0:
                        deps += ARN[tq].wdeps()
                    t_mm = Sc.add("pe", lambda e, tq=tq, tqg=tqg, st=st, dc=dc:
                                  e.matmul(
                        PSA[:, tq * 512:tq * 512 + MYKV],
                        sb_xn[:, dc * SM + (tqg * 4 + tq) * 128:
                              dc * SM + (tqg * 4 + tq + 1) * 128],
                        sb_wbf[st][:, 0:MYKV],
                        start=(dc == 0), stop=(dc == DC - 1)),
                        deps=deps, name="vmm")
                    if dc == 0:
                        ARN[tq].setw(t_mm)
                    B(f"wbfb{st}").rd(t_mm)
                    if dc == DC - 1:
                        mm_stop[tq] = t_mm
                        B("xnf").rd(t_mm)
            for tq in range(4):
                tg = th * TSUB + tqg * 4 + tq
                t_ev = Sc.add("act", lambda e, tq=tq, tg=tg: e.activation(
                    sb_V[:, tg * MYKV:(tg + 1) * MYKV],
                    PSA[:, tq * 512:tq * 512 + MYKV], AF.Copy, bias=0.0, scale=1.0),
                    deps=[mm_stop[tq]], name="vev")
                ARN[tq].rd(t_ev)
                v_fin = [t_ev]

    if STOP and (STOP in ("p1", "q", "k") or STOP.startswith("ln")):
        Sc.barrier("fin")
        return _emit()
    Sc.barrier("b3a")
    xn_evs = []
    for ts in range(TSUB):
        xn_evs += emit_ln_transpose(x_mine, ts * 128, ts * 128)
    B("xnf").setw(xn_evs[-1])
    xn_my = xn_evs[-1]

    Sc.barrier("b3b")
    t_rw = None
    for dc in range(DC):
        t_ld = dma(rWT[dc * 128:(dc + 1) * 128, :], sb_sst[:, 0:NE],
                   deps=B("sstb").wdeps(), name="rwld", sem_key="lds")
        B("sstb").setw(t_ld)
        t_rw = Sc.add("pool", lambda e, dc=dc: e.tensor_copy(
            sb_rwbf[:, dc * NE:(dc + 1) * NE], sb_sst[:, 0:NE]),
            deps=[t_ld], name="rwcv")
        B("sstb").rd(t_rw)
    t_bbf = None
    bw_ = min(1024, D)
    for j in range(max(1, D // 1024)):
        t_ld = dma(bb[:, j * bw_:(j + 1) * bw_], sb_bst[:, 0:bw_],
                   deps=B("bstb").wdeps(), name="bbld", sem_key="ldb")
        B("bstb").setw(t_ld)
        t_cv = Sc.add("pool", lambda e, bw_=bw_: e.tensor_copy(
            sb_bb16[:, 0:bw_], sb_bst[:, 0:bw_]),
            deps=[t_ld] + B("bb16b").wdeps(), name="bbcv")
        B("bb16b").setw(t_cv)
        B("bstb").rd(t_cv)
        t_bbf = Sc.add("pool", lambda e, j=j, bw_=bw_: e.dma_start(
            out=bb_bf_d[:, j * bw_:(j + 1) * bw_], in_=sb_bb16[:, 0:bw_]),
            deps=[t_cv], dma=True, name="bbst", sem_key="pst")
        B("bb16b").rd(t_bbf)
    t_ld = dma(rb[:, :], sb_bst[:, 0:NE], deps=B("bstb").wdeps(), name="rbld",
               sem_key="ldb")
    B("bstb").setw(t_ld)
    t_rbf = Sc.add("pool", lambda e: e.tensor_copy(
        sb_rbbf[:, 0:NE], sb_bst[:, 0:NE]), deps=[t_ld], name="rbcv")
    B("bstb").rd(t_rbf)
    t_emf = Sc.add("dve", lambda e: e.tensor_copy(sb_emf[:, :], sb_em[:, :]),
                   deps=[t_const], name="emf")
    t_mr0 = Sc.add("dve", lambda e: e.tensor_scalar(
        sb_eidx[:, :], sb_eidx[:, :], sb_emf[:, :], None, AL.is_equal),
        deps=[t_emf, t_const], name="mrep0")
    t_mr = Sc.add("act", lambda e: e.activation(
        sb_mrep[:, :], sb_eidx[:, :], AF.Copy, bias=0.0, scale=2.0),
        deps=[t_mr0], name="mrep")
    t_br = None
    for j in range(max(1, D // CW)):
        t_ld = dma(B_r[:, j * CW:(j + 1) * CW], sb_sst[:, 0:CW],
                   deps=B("sstb").wdeps(), name="brld", sem_key="lds")
        B("sstb").setw(t_ld)
        t_br = Sc.add("pool", lambda e, j=j: e.tensor_copy(
            sb_Brbf[:, j * CW:(j + 1) * CW], sb_sst[:, 0:CW]),
            deps=[t_ld], name="brcv")
        B("sstb").rd(t_br)

    t_mT = None
    for ts in range(TSUB):
        t_mm = None
        for dc in range(DC):
            deps = [t_rw, xn_my] + (AVB[0].wdeps() if dc == 0 else [])
            t_mm = Sc.add("pe", lambda e, dc=dc, ts=ts: e.matmul(
                AV0[:, 0:NE],
                sb_xn[:, dc * SM + ts * 128: dc * SM + (ts + 1) * 128],
                sb_rwbf[:, dc * NE:(dc + 1) * NE],
                start=(dc == 0), stop=False), deps=deps, name="lgmm")
            if dc == 0:
                AVB[0].setw(t_mm)
        t_mm = Sc.add("pe", lambda e: e.matmul(
            AV0[:, 0:NE], sb_ones[0:1, 0:128], sb_rbbf[0:1, 0:NE],
            start=False, stop=True), deps=[t_mm, t_rbf, t_const], name="lgbias")
        t_cm = Sc.add("dve", lambda e: e.tensor_scalar(
            sb_lg[:, :], AV0[:, 0:NE], 0.0, None, AL.add, AL.max,
            accum_out=sb_pm[:, :]), deps=[t_mm] + B("lgb").wdeps(), name="lgcm")
        B("lgb").setw(t_cm)
        AVB[0].rd(t_cm)
        t_nm = Sc.add("dve", lambda e: e.tensor_scalar(
            sb_pm[:, :], sb_pm[:, :], -1.0, None, AL.mult), deps=[t_cm], name="lgnm")
        t_ex = Sc.add("act", lambda e: e.activation(
            sb_pr[:, :], sb_lg[:, :], AF.Exp, bias=sb_pm[:, :], scale=1.0,
            accum_out=sb_ls[:, :]), deps=[t_nm] + B("prb").wdeps(), name="lgexp")
        B("lgb").rd(t_ex)
        t_rc = Sc.add("dve", lambda e: e.reciprocal(sb_rr[:, :], sb_ls[:, :]),
                      deps=[t_ex], name="lgrc")
        t_p = Sc.add("dve", lambda e: e.tensor_scalar(
            sb_pr[:, :], sb_pr[:, :], sb_rr[:, :], None, AL.mult),
            deps=[t_rc, t_ex], name="lgp")
        B("prb").setw(t_p)
        t_m8 = Sc.add("dve", lambda e: e.max(sb_t8[:, :], sb_pr[:, :]),
                      deps=[t_p], name="max8")
        t_ms = Sc.add("dve", lambda e: e.tensor_scalar(
            sb_msk[:, :], sb_pr[:, :], sb_t8[:, 7:8], None, AL.is_ge),
            deps=[t_m8], name="thr")
        t_mp = Sc.add("dve", lambda e: e.tensor_tensor(
            sb_mpb[:, :], sb_pr[:, :], sb_msk[:, :], AL.mult),
            deps=[t_ms] + B("mpbb").wdeps(), name="maskp")
        B("mpbb").setw(t_mp)
        B("prb").rd(t_mp)
        t_tr = Sc.add("pe", lambda e: e.transpose(
            PT0[0:NE, 0:128], sb_mpb[:, :], sb_ident[:, :]),
            deps=[t_mp, t_const] + PTB[0].wdeps(), name="mpT")
        PTB[0].setw(t_tr)
        B("mpbb").rd(t_tr)
        t_mT = Sc.add("act", lambda e, ts=ts: e.activation(
            sb_mT[:, ts * 128:(ts + 1) * 128], PT0[0:NE, 0:128], AF.Copy,
            bias=0.0, scale=1.0), deps=[t_tr], name="mpTe")
        PTB[0].rd(t_mT)

    t_wr = None
    for j in range(NTT):
        bank = 2 + j
        t_mm = Sc.add("pe", lambda e, j=j, bank=bank: e.matmul(
            PSA[:, bank * 512:(bank + 1) * 512],
            sb_mrep[:, :], sb_mT[:, j * 512:(j + 1) * 512],
            start=True, stop=True),
            deps=[t_mT, t_mr] + ARN[bank].wdeps(), name="wrmm")
        ARN[bank].setw(t_mm)
        t_wr = Sc.add("act", lambda e, j=j, bank=bank: e.activation(
            sb_wsb[:, j * 512:(j + 1) * 512], PSA[:, bank * 512:(bank + 1) * 512],
            AF.Copy, bias=0.0, scale=1.0), deps=[t_mm], name="wrev")
        ARN[bank].rd(t_wr)

    t_hw = None
    for tg in range(NTT):
        av = AVb[tg % 2]
        avb = AVB[tg % 2]
        t_mm = None
        for dc in range(DC):
            st = dc % 2
            t_ld = dma(A_T[dc * 128:(dc + 1) * 128, :], sb_sst[:, 0:ER],
                       deps=B("sstb").wdeps(), name="ald", sem_key="lds")
            B("sstb").setw(t_ld)
            t_cv = Sc.add("pool", lambda e, st=st: e.tensor_copy(
                sb_abf[st][:, :], sb_sst[:, 0:ER]),
                deps=[t_ld] + B(f"abfb{st}").wdeps(), name="acv")
            B(f"abfb{st}").setw(t_cv)
            B("sstb").rd(t_cv)
            deps = [t_cv, xn_my] + (avb.wdeps() if dc == 0 else [])
            t_mm = Sc.add("pe", lambda e, dc=dc, tg=tg, st=st, av=av: e.matmul(
                av[:, 0:512], sb_abf[st][:, :], xn_col(dc, tg * 512),
                start=(dc == 0), stop=(dc == DC - 1)),
                deps=deps, name="hmm")
            if dc == 0:
                avb.setw(t_mm)
            B(f"abfb{st}").rd(t_mm)
            if dc == DC - 1:
                B("xnf").rd(t_mm)
        t_hw = Sc.add("dve", lambda e, tg=tg, av=av: e.tensor_tensor(
            sb_hw[:, tg * 512:(tg + 1) * 512], av[:, 0:512],
            sb_wsb[:, tg * 512:(tg + 1) * 512], AL.mult),
            deps=[t_mm, t_wr], name="hwmul")
        avb.rd(t_hw)

    if STOP == "p3b":
        Sc.barrier("fin")
        return _emit()
    Sc.barrier("b2")
    QG = QT // 4
    HPC = max(1, NH // 4)
    EB = [B(f"eb{k}") for k in range(QT)]
    BCB = [B("bcb0"), B("bcb1")]
    SFB = [B("sfb0"), B("sfb1")]
    s_lasts = [None, None]
    cc_toks = []
    groups = [[2 * i, 2 * i + 1] for i in range(NCORES // 2)]
    idx = 0
    t_slast = None

    def attn_tail(st_):
        h, g, gp, t_ls, t_av = st_
        lb = 2 + gp
        t_le = Sc.add("act", lambda e, lb=lb: e.activation(
            sb_lrow[:, :], PSA[0:1, lb * 512:(lb + 1) * 512], AF.Copy,
            bias=0.0, scale=1.0),
            deps=[t_ls] + B("lrowb").wdeps(), name="lev")
        ARN[lb].rd(t_le)
        B("lrowb").setw(t_le)
        t_rc = Sc.add("dve", lambda e: e.reciprocal(sb_cf[:, :], sb_lrow[:, :]),
                      deps=[t_le] + B("cfb").wdeps(), name="crec")
        B("lrowb").rd(t_rc)
        B("cfb").setw(t_rc)
        t_cb = Sc.add("dve", lambda e: e.tensor_copy(sb_cb[:, :], sb_cf[:, :]),
                      deps=[t_rc] + B("cbb").wdeps(), name="ccvt")
        B("cfb").rd(t_cb)
        B("cbb").setw(t_cb)
        t_bc = Sc.add("pe", lambda e, lb=lb: e.matmul(
            PSA[:, lb * 512:(lb + 1) * 512], sb_ones[0:1, 0:128], sb_cb[0:1, :],
            start=True, stop=True),
            deps=[t_cb, t_const] + ARN[lb].wdeps(), name="bcmm")
        ARN[lb].setw(t_bc)
        B("cbb").rd(t_bc)
        t_be = Sc.add("act", lambda e, gp=gp, lb=lb: e.activation(
            sb_bc[gp][:, :], PSA[:, lb * 512:(lb + 1) * 512], AF.Copy,
            bias=0.0, scale=1.0),
            deps=[t_bc] + BCB[gp].wdeps(), name="bce")
        ARN[lb].rd(t_be)
        BCB[gp].setw(t_be)
        t_mu = Sc.add("dve", lambda e, gp=gp: e.tensor_tensor(
            sb_sf[gp][:, :], AVb[gp][:, 0:512], sb_bc[gp][:, :], AL.mult),
            deps=[t_av, t_be] + SFB[gp].wdeps(), name="smul")
        AVB[gp].rd(t_mu)
        BCB[gp].rd(t_mu)
        SFB[gp].setw(t_mu)
        t_st = dma(sb_sf[gp][:, :],
                   s_loc[h * 128:(h + 1) * 128, g * 512:(g + 1) * 512],
                   deps=[t_mu], name="sst", sem_key=f"sst{gp}")
        SFB[gp].rd(t_st)
        s_lasts[gp] = t_st
        return t_st

    pending = None
    for h in range(NH):
        kv = h // NREP
        for g in range(QG):
            nkb = 4 * g + 4
            gp = idx % 2
            idx += 1
            t_ls = None
            t_av = None
            lb = 2 + gp
            for kb in range(nkb):
                stb = kb % 2
                t_mm = Sc.add("pe", lambda e, stb=stb, kb=kb, h=h, g=g, kv=kv:
                              e.matmul(
                    PSA[:, stb * 512:(stb + 1) * 512],
                    sb_K[:, kv * S + kb * 128: kv * S + (kb + 1) * 128],
                    sb_Q[:, h * S + g * 512: h * S + (g + 1) * 512],
                    start=True, stop=True),
                    deps=q_fin + k_fin + ARN[stb].wdeps(), name="stmm")
                ARN[stb].setw(t_mm)
                t_exp = Sc.add("act", lambda e, stb=stb, kb=kb: e.activation(
                    sb_E[:, kb * 512:(kb + 1) * 512],
                    PSA[:, stb * 512:(stb + 1) * 512], AF.Exp,
                    bias=-EXPC, scale=ISQ),
                    deps=[t_mm] + EB[kb].wdeps(), name="eexp")
                ARN[stb].rd(t_exp)
                EB[kb].setw(t_exp)
                tE = t_exp
                if kb >= 4 * g:
                    j = kb - 4 * g
                    tE = Sc.add("dve", lambda e, kb=kb, j=j: e.tensor_tensor(
                        sb_E[:, kb * 512:(kb + 1) * 512],
                        sb_E[:, kb * 512:(kb + 1) * 512],
                        sb_stair[:, 384 - 128 * j: 896 - 128 * j], AL.mult),
                        deps=[t_exp, t_const], name="stair")
                    EB[kb].setw(tE)
                if kb == 1 and pending is not None:
                    t_st_prev = attn_tail(pending)
                    t_slast = t_st_prev
                    pending = None
                deps = [tE, t_const]
                if kb == 0:
                    deps += ARN[lb].wdeps()
                t_ls = Sc.add("pe", lambda e, kb=kb, nkb=nkb, lb=lb: e.matmul(
                    PSA[0:1, lb * 512:(lb + 1) * 512], sb_onec[:, 0:1],
                    sb_E[:, kb * 512:(kb + 1) * 512],
                    start=(kb == 0), stop=(kb == nkb - 1)),
                    deps=deps, name="lsmm")
                if kb == 0:
                    ARN[lb].setw(t_ls)
                EB[kb].rd(t_ls)
                deps = [tE] + v_fin
                if kb == 0:
                    deps += AVB[gp].wdeps()
                t_av = Sc.add("pe", lambda e, kb=kb, kv=kv, gp=gp, nkb=nkb:
                              e.matmul(
                    AVb[gp][:, 0:512],
                    sb_V[:, kb * MYKV + kv * 128: kb * MYKV + (kv + 1) * 128],
                    sb_E[:, kb * 512:(kb + 1) * 512],
                    start=(kb == 0), stop=(kb == nkb - 1)),
                    deps=deps, name="av")
                if kb == 0:
                    AVB[gp].setw(t_av)
                EB[kb].rd(t_av)
            if pending is not None:
                t_slast = attn_tail(pending)
            pending = (h, g, gp, t_ls, t_av)
        if (h + 1) % HPC == 0:
            k = h // HPC
            if pending is not None:
                t_slast = attn_tail(pending)
                pending = None
            r0 = k * HPC * 128
            r1 = (k + 1) * HPC * 128
            t_cck = Sc.add("pool", lambda e, r0=r0, r1=r1, k=k: e.collective_compute(
                "AllGather", AL.bypass, replica_groups=groups,
                ins=[s_loc[r0:r1, :].opt()],
                outs=[s_g[k][:, :].opt()]),
                deps=[t for t in s_lasts if t is not None],
                name="cc", sem_key=f"cc{k}")
            t_cck.needs = True
            cc_toks.append(t_cck)
    t_cc = cc_toks[-1]

    if STOP == "cc":
        Sc.add("pool", lambda e: e.nop(), deps=[t_cc], name="ccw")
        Sc.barrier("fin")
        return _emit()
    Sc.barrier("b3c")
    r_off = nc.sync.alloc_register("soff_reg")
    r_cur = nc.sync.alloc_register("soff_cur")
    t_off = Sc.add("sync", lambda e: e.reg_load(r_off, sb_soff[0:1, 0:1]),
                   deps=[t_const], name="soffld")
    t_sal = None
    for k in range(NCCH):
        for rk in range(2):
            dc0 = rk * NH + k * HPC
            base = rk * (HPC * 128) * S

            def _sald(e, base=base, dc0=dc0, k=k):
                e.reg_add(r_cur, r_off, base)
                dst = sb_sall[:, dc0 * SM:(dc0 + HPC) * SM].rearrange(
                    "p (j t) -> p j t", t=SM)
                return e.dma_start(
                    out=dst,
                    in_=bass.AP(s_g[k], r_cur,
                                [[S, 128], [128 * S, HPC], [1, SM]]),
                    bounds_check=None)

            t_sal = Sc.add("sync", _sald,
                           deps=[cc_toks[k], t_off, t_slast], dma=True,
                           name="sald", sem_key="lsal")
    t_sall = t_sal

    WQT = [B(f"wqt{i}") for i in range(8)]
    t_y_lasts = [None, None]
    for og in range(OG):
        t_wq = {}
        for w_i, (wsrc, wdst_sl, nrow) in enumerate(
                ((wo, sb_wosl, DCA), (bWT, sb_bwsl, DC))):
            for q4 in range(4):
                dc0 = q4 * (nrow // 4)
                ndc = nrow // 4
                qi = (w_i * 4 + q4) % 2
                bq = WQT[w_i * 4 + q4]
                bst = B(f"wfb{qi}")
                src_ap = wsrc[dc0 * 128:(dc0 + ndc) * 128, og * OGW:(og + 1) * OGW]
                src_ap = src_ap.rearrange("(d p) c -> p d c", p=128)
                dstf = sb_wf[qi][:, 0:ndc * OGW].rearrange(
                    "p (d c) -> p d c", c=OGW)
                deps = bst.wdeps() + ([t_cc] if og == 0 else [])
                t_ld = dma(src_ap, dstf, deps=deps, name="wsl",
                           sem_key=f"lwsl{qi}")
                bst.setw(t_ld)
                if (w_i * 4 + q4) % 2 == 0:
                    t_cv = Sc.add("dve", lambda e, qi=qi, wdst_sl=wdst_sl, dc0=dc0,
                                  ndc=ndc: e.tensor_copy(
                        wdst_sl[:, dc0 * OGW:(dc0 + ndc) * OGW],
                        sb_wf[qi][:, 0:ndc * OGW]),
                        deps=[t_ld] + bq.wdeps(), name="wslcv")
                else:
                    t_cv = Sc.add("act", lambda e, qi=qi, wdst_sl=wdst_sl, dc0=dc0,
                                  ndc=ndc: e.activation(
                        wdst_sl[:, dc0 * OGW:(dc0 + ndc) * OGW],
                        sb_wf[qi][:, 0:ndc * OGW], AF.Copy, bias=0.0, scale=1.0),
                        deps=[t_ld] + bq.wdeps(), name="wslcv")
                bq.setw(t_cv)
                bst.rd(t_cv)
                t_wq[(w_i, q4)] = t_cv
        bbb = B(f"bbsb{og % 2}")
        t_ld = dma(bb_bf_d[:, og * OGW:(og + 1) * OGW],
                   sb_bbs[0:1, (og % 2) * OGW:(og % 2 + 1) * OGW],
                   deps=bbb.wdeps() + ([t_bbf] if og < 2 else []),
                   name="bbsld", sem_key="lwb")
        bbb.setw(t_ld)
        t_wq_last = t_ld
        t_cv_all = [t_wq[(0, 3)], t_wq[(1, 3)], t_ld]
        for ts in range(TSUB):
            par = ts % 2
            ps0 = par * 512
            pg0 = 1024 + par * 512
            t_ms = None
            t_mg = None
            deps0 = ARN[par].wdeps() + [t_sall]
            for dc in range(max(DCA, DC)):
                if dc < DCA:
                    q4 = dc // (DCA // 4)
                    t_ms = Sc.add("pe", lambda e, dc=dc, ts=ts, ps0=ps0: e.matmul(
                        PSA[:, ps0:ps0 + OGW],
                        sb_sall[:, dc * SM + ts * 128: dc * SM + (ts + 1) * 128],
                        sb_wosl[:, dc * OGW:(dc + 1) * OGW],
                        start=(dc == 0), stop=(dc == DCA - 1)),
                        deps=(deps0 if dc == 0 else []) + [t_wq[(0, q4)]],
                        name="omm")
                    if dc == 0:
                        ARN[par].setw(t_ms)
                    if ts == TSUB - 1 and dc % (DCA // 4) == (DCA // 4) - 1:
                        WQT[q4].rd(t_ms)
                if dc < DC:
                    q4 = dc // (DC // 4)
                    t_mg = Sc.add("pe", lambda e, dc=dc, ts=ts, pg0=pg0: e.matmul(
                        PSA[:, pg0:pg0 + OGW],
                        sb_xn[:, dc * SM + ts * 128: dc * SM + (ts + 1) * 128],
                        sb_bwsl[:, dc * OGW:(dc + 1) * OGW],
                        start=(dc == 0), stop=False),
                        deps=([xn_my] + ARN[2 + par].wdeps() if dc == 0 else [])
                        + [t_wq[(1, q4)]], name="gmm")
                    if dc == 0:
                        ARN[2 + par].setw(t_mg)
                    if ts == TSUB - 1 and dc % (DC // 4) == (DC // 4) - 1:
                        WQT[4 + q4].rd(t_mg)
            t_mg = Sc.add("pe", lambda e, og=og, pg0=pg0: e.matmul(
                PSA[:, pg0:pg0 + OGW], sb_ones[0:1, 0:128],
                sb_bbs[0:1, (og % 2) * OGW:(og % 2 + 1) * OGW],
                start=False, stop=False),
                deps=[t_mg, t_wq_last, t_const], name="gbias")
            B(f"bbsb{og % 2}").rd(t_mg)
            t_mg = Sc.add("pe", lambda e, og=og, ts=ts, pg0=pg0: e.matmul(
                PSA[:, pg0:pg0 + OGW], sb_hw[:, ts * 128:(ts + 1) * 128],
                sb_Brbf[:, og * OGW:(og + 1) * OGW], start=False, stop=True),
                deps=[t_mg, t_hw, t_br], name="glora")
            t_g = Sc.add("act", lambda e, par=par, pg0=pg0: e.activation(
                sb_g[par][:, :], PSA[:, pg0:pg0 + OGW], AF.Silu,
                bias=0.0, scale=1.0), deps=[t_mg] + B(f"gb{par}").wdeps(),
                name="silu")
            B(f"gb{par}").setw(t_g)
            ARN[2 + par].rd(t_g)
            t_xr = dma(x_mine[ts * 128:(ts + 1) * 128, og * OGW:(og + 1) * OGW],
                       sb_xr[par][:, :], deps=B(f"xrb{par}").wdeps(), name="xrld",
                       sem_key=f"lxr{par}")
            B(f"xrb{par}").setw(t_xr)
            t_sg = Sc.add("dve", lambda e, par=par, ps0=ps0: e.tensor_tensor(
                sb_sg[par][:, :], PSA[:, ps0:ps0 + OGW], sb_g[par][:, :], AL.mult),
                deps=[t_ms, t_g] + B(f"sgb{par}").wdeps(), name="sgmul")
            B(f"sgb{par}").setw(t_sg)
            B(f"gb{par}").rd(t_sg)
            ARN[par].rd(t_sg)
            t_o = Sc.add("dve", lambda e, par=par: e.tensor_tensor(
                sb_out[par][:, :], sb_sg[par][:, :], sb_xr[par][:, :], AL.add),
                deps=[t_sg, t_xr] + B(f"ob{par}").wdeps(), name="oadd")
            B(f"ob{par}").setw(t_o)
            B(f"sgb{par}").rd(t_o)
            B(f"xrb{par}").rd(t_o)
            t_y = dma(sb_out[par][:, :],
                      y[ts * 128:(ts + 1) * 128, og * OGW:(og + 1) * OGW],
                      deps=[t_o], name="yst", sem_key=f"syt{par}")
            B(f"ob{par}").rd(t_y)
            t_y_lasts[par] = t_y

    Sc.add("sync", lambda e: e.nop(), deps=t_y_lasts, name="final_sync")
    Sc.add("pool", lambda e: e.nop(), deps=t_y_lasts + [t_bbf, t_cc],
           name="final_pool")

    return _emit()


def _host_consts(cfg):
    S, NE, R, E = cfg["S"], cfg["NE"], cfg["R"], cfg["E"]
    bf = ml_dtypes.bfloat16
    ident = np.eye(128, dtype=np.float32).astype(bf)
    tri = np.triu(np.full((128, 128), NEG, dtype=np.float32), k=1)
    freqs = (1.0 / (10000.0 ** (np.arange(0, 128, 2, dtype=np.float32)
                                / np.float32(128.0)))).astype(np.float32)
    t = np.arange(S, dtype=np.float32)
    ang = np.outer(t, freqs).astype(np.float32)
    cos_fm = np.cos(ang).astype(np.float32).T
    sin_fm = np.sin(ang).astype(np.float32).T
    cos2 = np.ascontiguousarray(np.vstack([cos_fm, cos_fm]))
    sinf = np.ascontiguousarray(np.vstack([-sin_fm, sin_fm]))
    swap = np.zeros((128, 128), dtype=np.float32)
    swap[(np.arange(128) + 64) % 128, np.arange(128)] = 1.0
    swap = np.ascontiguousarray(swap.astype(bf))
    eidx = np.ascontiguousarray(
        np.tile((np.arange(E * R, dtype=np.float32) // R)[None, :], (NE, 1)))
    ones = np.ones((1, 512), dtype=bf)
    onec = np.ones((128, 1), dtype=bf)
    kk = np.arange(128)[:, None]
    cc = np.arange(896)[None, :]
    stair = (kk <= cc - 384).astype(np.float32).astype(bf)
    f32s = np.zeros((128, 16), dtype=np.float32)
    vals = [0.0, 1.0, 1e-5, 1.0 / math.sqrt(128.0), 2.0, -12.0]
    f32s[:, :len(vals)] = np.asarray(vals, dtype=np.float32)[None, :]
    return dict(c_ident=ident, c_tri=tri, c_cos2=cos2, c_sinf=sinf,
                c_swap=swap, c_eidx=eidx, c_ones=ones, c_f32s=f32s,
                c_stair=np.ascontiguousarray(stair), c_onec=onec)


def _deinterleave(w, nheads):
    d = w.shape[0]
    return np.ascontiguousarray(
        w.reshape(d, nheads, 64, 2).transpose(0, 1, 3, 2).reshape(d, nheads * 128))


def make_in_maps(cfg, L0_attn_x, L0_attn_wq, L0_attn_wk, L0_attn_wv, L0_attn_wo,
                 L0_ffn_base_W, L0_ffn_base_b, L0_ffn_router_W, L0_ffn_router_b,
                 L0_ffn_A, L0_ffn_B, L0_ffn_expert_map):
    S, D, NH, NKV, NCORES = cfg["S"], cfg["D"], cfg["NH"], cfg["NKV"], cfg["NCORES"]
    SM, MYQ, MYKV = S // 2, NH * 128, NKV * 128
    f = np.ascontiguousarray
    x = np.asarray(L0_attn_x, dtype=np.float32)
    wq_d = _deinterleave(np.asarray(L0_attn_wq, np.float32), 2 * NH)
    wk_d = _deinterleave(np.asarray(L0_attn_wk, np.float32), 2 * NKV)
    wv = np.asarray(L0_attn_wv, np.float32)
    wo_ = f(np.asarray(L0_attn_wo, np.float32))
    bWT = f(np.asarray(L0_ffn_base_W, np.float32).T)
    bb = f(np.asarray(L0_ffn_base_b, np.float32).reshape(1, D))
    rWT = f(np.asarray(L0_ffn_router_W, np.float32).T)
    rb = f(np.asarray(L0_ffn_router_b, np.float32).reshape(1, -1))
    A_T = f(np.asarray(L0_ffn_A, np.float32).reshape(128, D).T)
    B_rr = f(np.asarray(L0_ffn_B, np.float32).transpose(0, 2, 1).reshape(128, D))
    em_ = f(np.asarray(L0_ffn_expert_map, np.int32).reshape(-1, 1))
    consts = _host_consts(cfg)
    maps = []
    for c in range(NCORES):
        b, hg = c // 2, c % 2
        m = dict(consts)
        m["x_b"] = f(x[b])
        m["x_mine"] = f(x[b, hg * SM:(hg + 1) * SM])
        m["wq_de"] = f(wq_d[:, hg * MYQ:(hg + 1) * MYQ])
        m["wk_de"] = f(wk_d[:, hg * MYKV:(hg + 1) * MYKV])
        m["wv_sl"] = f(wv[:, hg * MYKV:(hg + 1) * MYKV])
        m["wo"] = wo_
        m["bWT"] = bWT
        m["bb"] = bb
        m["rWT"] = rWT
        m["rb"] = rb
        m["A_T"] = A_T
        m["B_r"] = B_rr
        m["em"] = em_
        m["c_soff"] = np.array([[hg * SM]], dtype=np.int32)
        maps.append(m)
    return maps


_NC_CACHE = {}


def kernel(**inputs):
    cfg = make_cfg(full=True)
    if "full" not in _NC_CACHE:
        _NC_CACHE["full"] = build_nc(cfg)
    nc = _NC_CACHE["full"]
    in_maps = make_in_maps(cfg, **inputs)
    res = run_bass_kernel_spmd(nc, in_maps, core_ids=list(range(cfg["NCORES"])))
    S, D = cfg["S"], cfg["D"]
    SM = S // 2
    out = np.empty((4, S, D), dtype=np.float32)
    for c in range(cfg["NCORES"]):
        b, hg = c // 2, c % 2
        out[b, hg * SM:(hg + 1) * SM] = res.results[c]["y"]
    return out


# version 10
# speedup vs baseline: 1.3595x; 1.3595x over previous
import contextlib
import math

import numpy as np
import ml_dtypes

import concourse.bass as bass
import concourse.mybir as mybir
from concourse.bass_utils import run_bass_kernel_spmd
from concourse.aot_env import allow_ici_all_to_all

F32 = mybir.dt.float32
BF16 = mybir.dt.bfloat16
I32 = mybir.dt.int32
AL = mybir.AluOpType
AF = mybir.ActivationFunctionType
AX = mybir.AxisListType

NEG = -1.0e30


def make_cfg(full=True):
    if full:
        return dict(S=2048, D=4096, NH=16, NKV=4, R=16, E=8, NE=15, NCORES=8)
    return dict(S=1024, D=512, NH=4, NKV=1, R=16, E=8, NE=15, NCORES=2)


class Task:
    __slots__ = ("eng", "fn", "deps", "inc", "needs", "value", "name", "sem_key",
                 "barwait")

    def __init__(self, eng, fn, deps, inc, name, sem_key):
        self.eng = eng
        self.fn = fn
        self.deps = deps
        self.inc = inc
        self.needs = inc == 16
        self.value = None
        self.name = name
        self.sem_key = sem_key
        self.barwait = None


class Buf:
    __slots__ = ("w", "rs")

    def __init__(self):
        self.w = None
        self.rs = []

    def wdeps(self):
        d = list(self.rs)
        if self.w is not None:
            d.append(self.w)
        return d

    def setw(self, t):
        self.w = t
        self.rs = []

    def rdep(self):
        return [self.w] if self.w is not None else []

    def rd(self, t):
        self.rs.append(t)


class Sched:
    ENGINES = ("pe", "dve", "act", "pool", "sync")

    def __init__(self):
        self.tasks = []
        self.last = {}
        self.lastsem = {}
        self.nbar = 0

    def add(self, eng, fn, deps=(), dma=False, name="", sem_key=None):
        ds = [d for d in deps if d is not None]
        t = Task(eng, fn, ds, 16 if dma else 1, name, sem_key or eng)
        for d in ds:
            if not (d.eng == eng == "pe" and d.inc == 1):
                d.needs = True
        self.tasks.append(t)
        self.last[eng] = t
        self.lastsem[t.sem_key] = t
        return t

    def barrier(self, name="bar"):
        self.nbar += 1
        engines = [e for e in self.ENGINES if e in self.last]
        target = self.nbar * len(engines)
        for e in engines:
            deps = [t for t in self.lastsem.values()
                    if t.eng == e and t.inc == 16]
            t = self.add(e, lambda eng: eng.nop(), deps=deps, name=name,
                         sem_key="bar")
            t.needs = True
        for e in engines:
            w = Task(e, lambda eng: None, [], 1, name + "w", e)
            w.barwait = target
            self.tasks.append(w)
            self.last[e] = w

    def finalize(self):
        cum = {}
        for t in self.tasks:
            if t.barwait is not None:
                continue
            if t.eng != "pe":
                t.needs = True
            if t.needs:
                cum[t.sem_key] = cum.get(t.sem_key, 0) + t.inc
                t.value = cum[t.sem_key]
        return sorted(cum.keys())

    def replay(self, eng_key, eng, sems):
        wm = {}
        for t in self.tasks:
            if t.eng != eng_key:
                continue
            bw = getattr(t, "barwait", None)
            if bw is not None:
                eng.wait_ge(sems["bar"], bw)
                continue
            for d in t.deps:
                if d.eng == eng_key == "pe" and d.inc == 1:
                    continue
                if d.inc == 16 and t.inc == 16 and d.sem_key == t.sem_key:
                    continue
                if d.value is None:
                    continue
                if wm.get(d.sem_key, 0) < d.value:
                    eng.wait_ge(sems[d.sem_key], d.value)
                    wm[d.sem_key] = d.value
            inst = t.fn(eng)
            if t.needs:
                assert inst is not None, f"task {t.name} needs inc but emitted nothing"
                inst.then_inc(sems[t.sem_key], t.inc)


class SAlloc:
    def __init__(self, nc):
        self.nc = nc
        self.base = ((nc.sbuf_base + 63) // 64) * 64
        self.top = nc.sbuf_top
        self.cur = self.base

    def take(self, name, shape, dtype, align=64):
        self.cur = ((self.cur + align - 1) // align) * align
        per = int(np.prod(shape[1:])) * mybir.dt.size(dtype)
        assert self.cur + per <= self.top, (
            f"SBUF overflow at {name}: {self.cur}+{per} > {self.top}"
        )
        t = self.nc.alloc_sbuf_tensor_at(name, list(shape), dtype, offset=self.cur)
        self.cur += per
        return t

    def mark(self):
        return self.cur

    def reset(self, m):
        self.cur = m


def build_nc(cfg):
    S, D, NH, NKV = cfg["S"], cfg["D"], cfg["NH"], cfg["NKV"]
    R, E, NE = cfg["R"], cfg["E"], cfg["NE"]
    NCORES = cfg["NCORES"]
    SM = S // 2
    DC = D // 128
    MYQ = NH * 128
    MYKV = NKV * 128
    DATT = 2 * NH * 128
    DCA = DATT // 128
    QT = S // 128
    TSUB = SM // 128
    NREP = 4
    ER = E * R
    ISQ = 1.0 / math.sqrt(128.0)
    OGW = 256
    OG = D // OGW
    EPS = 1e-5
    NTT = SM // 512
    CW = min(2048, D)
    assert SM % 512 == 0 and TSUB % 4 == 0 and DC % 4 == 0 and ER == 128
    assert DCA % 4 == 0 and NH * S == DCA * SM

    nc = bass.Bass("TRN2", target_bir_lowering=False)

    x_b = nc.dram_tensor("x_b", [S, D], F32, kind="ExternalInput")
    x_mine = nc.dram_tensor("x_mine", [SM, D], F32, kind="ExternalInput")
    wq_de = nc.dram_tensor("wq_de", [D, MYQ], F32, kind="ExternalInput")
    wk_de = nc.dram_tensor("wk_de", [D, MYKV], F32, kind="ExternalInput")
    wv_sl = nc.dram_tensor("wv_sl", [D, MYKV], F32, kind="ExternalInput")
    wo = nc.dram_tensor("wo", [DATT, D], F32, kind="ExternalInput")
    bWT = nc.dram_tensor("bWT", [D, D], F32, kind="ExternalInput")
    bb = nc.dram_tensor("bb", [1, D], F32, kind="ExternalInput")
    rWT = nc.dram_tensor("rWT", [D, NE], F32, kind="ExternalInput")
    rb = nc.dram_tensor("rb", [1, NE], F32, kind="ExternalInput")
    A_T = nc.dram_tensor("A_T", [D, ER], F32, kind="ExternalInput")
    B_r = nc.dram_tensor("B_r", [ER, D], F32, kind="ExternalInput")
    em = nc.dram_tensor("em", [NE, 1], I32, kind="ExternalInput")
    c_ident = nc.dram_tensor("c_ident", [128, 128], BF16, kind="ExternalInput")
    c_tri = nc.dram_tensor("c_tri", [128, 128], F32, kind="ExternalInput")
    c_cos2 = nc.dram_tensor("c_cos2", [128, S], F32, kind="ExternalInput")
    c_sinf = nc.dram_tensor("c_sinf", [128, S], F32, kind="ExternalInput")
    c_swap = nc.dram_tensor("c_swap", [128, 128], BF16, kind="ExternalInput")
    c_eidx = nc.dram_tensor("c_eidx", [NE, 128], F32, kind="ExternalInput")
    c_ones = nc.dram_tensor("c_ones", [1, 512], BF16, kind="ExternalInput")
    c_f32s = nc.dram_tensor("c_f32s", [128, 16], F32, kind="ExternalInput")
    c_soff = nc.dram_tensor("c_soff", [1, 1], I32, kind="ExternalInput")
    y = nc.dram_tensor("y", [SM, D], F32, kind="ExternalOutput")

    s_loc = nc.dram_tensor("s_loc", [S, MYQ], BF16)
    s_gath = nc.dram_tensor("s_gath", [2 * S, MYQ], BF16)
    bb_bf_d = nc.dram_tensor("bb_bf_d", [1, D], BF16)

    PSA = nc.alloc_psum_tensor("psA", [128, 2048], F32)
    AV0 = nc.alloc_psum_tensor("psAV0", [128, 512], F32)
    AV1 = nc.alloc_psum_tensor("psAV1", [128, 512], F32)
    PT0 = nc.alloc_psum_tensor("psPT0", [128, 512], BF16)
    PT1 = nc.alloc_psum_tensor("psPT1", [128, 512], BF16)
    PTb = [PT0, PT1]
    AVb = [AV0, AV1]

    sa = SAlloc(nc)
    sb_ident = sa.take("ident", [128, 128], BF16)
    sb_tri = sa.take("tri", [128, 128], F32)
    sb_f32s = sa.take("f32s", [128, 16], F32)
    sb_soff = sa.take("soff", [1, 1], I32)
    ACT_CONSTS = [0.0, 1.0, EPS, ISQ, 2.0]
    for _i, _v in enumerate(ACT_CONSTS):
        nc.const_aps.register(F32, float(_v), sb_f32s[:, _i:_i + 1])
    sb_ones = sa.take("ones", [1, 512], BF16)
    sb_eidx = sa.take("eidx", [NE, 128], F32)
    sb_em = sa.take("em", [NE, 1], I32)
    sb_emf = sa.take("emf", [NE, 1], F32)
    sb_mrep = sa.take("mrep", [NE, 128], BF16)
    sb_swap = sa.take("swap", [128, 128], BF16)
    sb_Q = sa.take("Qsb", [128, NH * S], BF16)
    sb_K = sa.take("Ksb", [128, NKV * S], BF16)
    sb_V = sa.take("Vsb", [128, (S // 128) * MYKV], BF16)
    sb_xn = sa.take("xnf", [128, DC * SM], BF16)
    sb_sall = nc.alloc_sbuf_tensor_at(
        "sall", [128, DCA * SM], BF16, offset=nc.lookup_mloc(sb_Q).addr)
    if NKV * S * 2 >= DCA * OGW * 2 and (S // 128) * MYKV * 2 >= DC * OGW * 2:
        sb_wosl = nc.alloc_sbuf_tensor_at(
            "wosl", [128, DCA * OGW], BF16, offset=nc.lookup_mloc(sb_K).addr)
        sb_bwsl = nc.alloc_sbuf_tensor_at(
            "bwsl", [128, DC * OGW], BF16, offset=nc.lookup_mloc(sb_V).addr)
    else:
        sb_wosl = sa.take("wosl", [128, DCA * OGW], BF16)
        sb_bwsl = sa.take("bwsl", [128, DC * OGW], BF16)

    W0 = sa.mark()
    sb_cosh = sa.take("cosh", [128, SM], F32)
    sb_sinh = sa.take("sinh", [128, SM], F32)
    sb_xst2 = [sa.take(f"xst{i}", [128, max(512, D // 2)], F32) for i in range(2)]
    sb_xtm = sa.take("xtm", [128, 1024], BF16)
    sb_bn = sa.take("bn", [128, 6 * (D // 512 if D >= 512 else 1)], F32)
    sb_mv = sa.take("mv", [128, 2], F32)
    sb_sd = sa.take("sd", [128, 1], F32)
    sb_rstd = sa.take("rstd", [128, 1], F32)
    sb_negmr = sa.take("negmr", [128, 1], F32)
    sb_wst = [sa.take(f"wst{i}", [128, 512], F32) for i in range(4)]
    sb_wbf = [sa.take(f"wbf{i}", [128, 512], BF16) for i in range(4)]
    sb_qb = sa.take("qb", [128, 512], BF16)
    sb_ub = sa.take("ub", [128, 512], BF16)
    sb_rt = [sa.take(f"rt{i}", [128, 512], F32) for i in range(2)]
    p1_end = sa.mark()

    sa.reset(W0)
    sb_Brbf = sa.take("Brbf", [128, D], BF16)
    sb_hw = sa.take("hw", [128, SM], BF16)
    W1 = sa.mark()
    sb_rbbf = sa.take("rbbf", [1, 64], BF16)
    sb_bb16 = sa.take("bb16", [1, 1024], BF16)
    sb_rwbf = sa.take("rwbf", [128, NE * DC], BF16)
    sb_abf = [sa.take(f"abf{i}", [128, ER], BF16) for i in range(2)]
    sb_sst = sa.take("sst", [128, CW], F32)
    sb_bst = sa.take("bst", [1, 1024], F32)
    sb_lg = sa.take("lg", [128, NE], F32)
    sb_pm = sa.take("pm", [128, 1], F32)
    sb_ls = sa.take("ls", [128, 1], F32)
    sb_rr = sa.take("rr3", [128, 1], F32)
    sb_pr = sa.take("pr", [128, NE], F32)
    sb_t8 = sa.take("t8", [128, 8], F32)
    sb_msk = sa.take("msk", [128, NE], F32)
    sb_mpb = sa.take("mpb", [128, NE], BF16)
    sb_mT = sa.take("mT", [NE, SM], BF16)
    sb_wsb = sa.take("wsb", [128, SM], F32)
    p3b_end = sa.mark()

    sa.reset(W1)
    sb_Ssb = [sa.take(f"Ssb{i}", [128, S], F32) for i in range(2)]
    sb_P = [sa.take(f"Psb{i}", [128, S], BF16) for i in range(2)]
    sb_PT = sa.take("PTsb", [128, 1024], BF16)
    _st2 = [sa.take(f"st2_{i}", [128, 8], F32) for i in range(2)]
    sb_mp = [t[:, 0:4] for t in _st2]
    sb_nm = [t[:, 4:5] for t in _st2]
    sb_l = [t[:, 5:6] for t in _st2]
    sb_r = [t[:, 6:7] for t in _st2]
    sb_ssb = [sa.take(f"ssb{i}", [128, 128], BF16) for i in range(2)]
    p2_end = sa.mark()

    sa.reset(W1)
    sb_sxst = [sa.take(f"sxst{i}", [128, MYQ], BF16) for i in range(2)]
    sb_bbs = sa.take("bbs", [1, 512], BF16)
    sb_wf = [sa.take(f"wf{i}", [128, 2048], F32) for i in range(2)]
    sb_g = [sa.take(f"g{i}", [128, OGW], F32) for i in range(2)]
    sb_sg = [sa.take(f"sg{i}", [128, OGW], F32) for i in range(2)]
    sb_xr = [sa.take(f"xr{i}", [128, OGW], F32) for i in range(2)]
    sb_out = [sa.take(f"out{i}", [128, OGW], F32) for i in range(2)]
    p3c_end = sa.mark()

    Sc = Sched()
    bufs = {}

    def B(name):
        if name not in bufs:
            bufs[name] = Buf()
        return bufs[name]

    ARN = [B(f"arn{k}") for k in range(4)]
    PTB = [B("ptb0"), B("ptb1")]
    AVB = [B("avb0"), B("avb1")]
    STOP = cfg.get("stop")

    def _emit():
        for _t in Sc.tasks:
            if _t.eng == "act":
                _t.deps.append(t_const)
        sem_keys = Sc.finalize()
        stack = contextlib.ExitStack()
        sems = {k: stack.enter_context(nc.semaphore(f"sem_{k}")) for k in sem_keys}
        with stack:
            with nc.Block() as block:
                @block.tensor
                def _(eng):
                    Sc.replay("pe", eng, sems)

                @block.vector
                def _(eng):
                    Sc.replay("dve", eng, sems)

                @block.scalar
                def _(eng):
                    Sc.replay("act", eng, sems)

                @block.gpsimd
                def _(eng):
                    with allow_ici_all_to_all():
                        Sc.replay("pool", eng, sems)

                @block.sync
                def _(eng):
                    Sc.replay("sync", eng, sems)
        return nc

    def dma(src_ap, dst_ap, deps=(), name="dma", q="sync", sem_key=None):
        return Sc.add(q, lambda e, o=dst_ap, i=src_ap: e.dma_start(out=o, in_=i),
                      deps=deps, dma=True, name=name, sem_key=sem_key)

    def act_copy(eng_dst_src):
        pass

    t_c = None
    for src, dst in ((c_f32s[:, :], sb_f32s[:, :]),
                     (c_ident[:, :], sb_ident[:, :]), (c_tri[:, :], sb_tri[:, :]),
                     (c_ones[:, :], sb_ones[:, :]), (c_eidx[:, :], sb_eidx[:, :]),
                     (em[:, :], sb_em[:, :]), (c_soff[:, :], sb_soff[:, :]),
                     (c_swap[:, :], sb_swap[:, :])):
        t_c = dma(src, dst, name="const", sem_key="ldc")
    t_const = t_c

    if STOP == "c0":
        Sc.barrier("fin")
        return _emit()

    def xn_col(dc, t0):
        return sb_xn[:, dc * SM + t0: dc * SM + t0 + 512]

    def emit_ln_transpose(xsrc, row0, dstcol):
        bxt = B("xtm")
        hw2 = max(512, D // 2)
        t_lds = []
        for hf in range(2 if D > 512 else 1):
            bx = B(f"xstb{hf}")
            t_ld = dma(xsrc[row0:row0 + 128, hf * hw2:(hf + 1) * hw2],
                       sb_xst2[hf][:, :], deps=bx.wdeps(),
                       name="xld", sem_key=f"ldx{hf}")
            bx.setw(t_ld)
            t_lds.append(t_ld)
        if len(t_lds) == 1:
            t_lds.append(t_lds[0])
        nbn = max(1, D // 512)
        t_bn = None
        wdep = B("bn").wdeps()
        for i in range(nbn):
            w = min(512, D)
            hf = 0 if i < nbn // 2 or D <= 512 else 1
            co = i * 512 - hf * hw2
            t_bn = Sc.add("dve", lambda e, i=i, w=w, hf=hf, co=co: e.bn_stats(
                sb_bn[:, i * 6:(i + 1) * 6], sb_xst2[hf][:, co:co + w]),
                deps=[t_lds[hf]] + wdep, name="bnst")
            B(f"xstb{hf}").rd(t_bn)
        B("bn").setw(t_bn)
        if STOP and STOP.startswith("ln_ld"):
            return [t_bn]
        t_ag = Sc.add("dve", lambda e: e.bn_aggr(sb_mv[:, :], sb_bn[:, 0:nbn * 6]),
                      deps=[t_bn] + B("mv").wdeps(), name="bnag")
        B("mv").setw(t_ag)
        B("bn").rd(t_ag)
        if STOP and STOP.startswith("ln_bn"):
            return [t_ag]
        t_sd = Sc.add("act", lambda e: e.activation(
            sb_sd[:, :], sb_mv[:, 1:2], AF.Sqrt, bias=EPS, scale=1.0),
            deps=[t_ag] + B("sdv").wdeps(), name="sd")
        B("sdv").setw(t_sd)
        B("mv").rd(t_sd)
        t_rs = Sc.add("dve", lambda e: e.reciprocal(sb_rstd[:, :], sb_sd[:, :]),
                      deps=[t_sd] + B("rstdv").wdeps(), name="rstd")
        B("rstdv").setw(t_rs)
        B("sdv").rd(t_rs)
        t_nm = Sc.add("dve", lambda e: e.scalar_tensor_tensor(
            sb_negmr[:, :], sb_mv[:, 0:1], -1.0, sb_rstd[:, :], AL.mult, AL.mult),
            deps=[t_rs] + B("negv").wdeps(), name="negmr")
        B("negv").setw(t_nm)
        B("mv").rd(t_nm)
        B("rstdv").rd(t_nm)
        if STOP and STOP.startswith("ln_sd"):
            return [t_nm]
        evs = []
        nhalf = max(1, D // 1024)
        hw_ = min(1024, D)
        for hf in range(nhalf):
            xsl = 0 if (hf * hw_) < hw2 or D <= 512 else 1
            xco = hf * hw_ - xsl * hw2
            t_xn = Sc.add("act", lambda e, hw_=hw_, xsl=xsl, xco=xco: e.activation(
                sb_xtm[:, 0:hw_], sb_xst2[xsl][:, xco:xco + hw_], AF.Identity,
                bias=sb_negmr[:, :], scale=sb_rstd[:, :]),
                deps=[t_nm, t_lds[xsl]] + bxt.wdeps(), name="xnorm")
            bxt.setw(t_xn)
            B(f"xstb{xsl}").rd(t_xn)
            B("rstdv").rd(t_xn)
            B("negv").rd(t_xn)
            if STOP and STOP.startswith("ln_no"):
                evs.append(t_xn)
                continue
            ndg = hw_ // 512
            for dg in range(ndg):
                pb = dg % 2
                tts = []
                for i in range(4):
                    dcl = dg * 4 + i
                    dc = hf * 16 + dcl
                    tts.append(Sc.add("pe", lambda e, dcl=dcl, pb=pb, i=i: e.transpose(
                        PTb[pb][:, i * 128:(i + 1) * 128],
                        sb_xtm[:, dcl * 128:(dcl + 1) * 128], sb_ident[:, :]),
                        deps=[t_xn, t_const] + (PTB[pb].wdeps() if i == 0 else []),
                        name="xT"))
                PTB[pb].setw(tts[-1])
                bxt.rd(tts[-1])
                ap = sb_xn[:, :].rearrange("p (d t) -> p d t", t=SM)
                dc0 = (hf * hw_) // 128 + dg * 4
                dst = ap[:, dc0:dc0 + 4, dstcol:dstcol + 128]
                t_ev = Sc.add("act", lambda e, pb=pb, dst=dst: e.activation(
                    dst, PTb[pb][:, :], AF.Copy, bias=0.0, scale=1.0),
                    deps=[tts[-1]] + B("xnf").wdeps(), name="xTe")
                PTB[pb].rd(t_ev)
                evs.append(t_ev)
        return evs

    def stream_chunk(hbm_ap, w, st, name):
        bst, bcv = B(f"wstb{st}"), B(f"wbfb{st}")
        t_ld = dma(hbm_ap, sb_wst[st][:, 0:w], deps=bst.wdeps(), name=name,
                   sem_key=f"ldw{st}")
        bst.setw(t_ld)
        eng = "pool" if st % 2 == 0 else "dve"
        t_cv = Sc.add(eng, lambda e, st=st, w=w: e.tensor_copy(
            sb_wbf[st][:, 0:w], sb_wst[st][:, 0:w]), deps=[t_ld] + bcv.wdeps(),
            name=name + "c")
        bcv.setw(t_cv)
        bst.rd(t_cv)
        return t_cv

    def rope_evac(mm_stop, slck, col, th, tt, dst_sb, t_tab):
        bq, bu = B("qbb"), B("ubb")
        t_cp = Sc.add("act", lambda e, slck=slck: e.activation(
            sb_qb[:, :], PSA[:, slck * 512:(slck + 1) * 512], AF.Copy,
            bias=0.0, scale=1.0), deps=[mm_stop] + bq.wdeps(), name="rcp")
        bq.setw(t_cp)
        ARN[slck].rd(t_cp)
        pb = slck % 2
        t_sw = Sc.add("pe", lambda e, pb=pb: e.matmul(
            AVb[pb][:, :], sb_swap[:, :], sb_qb[:, :], start=True, stop=True),
            deps=[t_cp, t_const] + AVB[pb].wdeps(), name="rsw")
        AVB[pb].setw(t_sw)
        bq.rd(t_sw)
        t_ue = Sc.add("act", lambda e, pb=pb: e.activation(
            sb_ub[:, :], AVb[pb][:, :], AF.Copy, bias=0.0, scale=1.0),
            deps=[t_sw] + bu.wdeps(), name="rue")
        bu.setw(t_ue)
        AVB[pb].rd(t_ue)
        c0 = tt * 512
        csl = sb_cosh[:, c0:c0 + 512]
        ssl = sb_sinh[:, c0:c0 + 512]
        t1 = Sc.add("dve", lambda e, csl=csl: e.tensor_tensor(
            sb_rt[0][:, :], sb_qb[:, :], csl, AL.mult),
            deps=[t_cp, t_tab] + B("rtb0").wdeps(), name="rpA")
        B("rtb0").setw(t1)
        bq.rd(t1)
        t2 = Sc.add("pool", lambda e, ssl=ssl: e.tensor_tensor(
            sb_rt[1][:, :], sb_ub[:, :], ssl, AL.mult),
            deps=[t_ue, t_tab] + B("rtb1").wdeps(), name="rpB")
        B("rtb1").setw(t2)
        bu.rd(t2)
        tro = Sc.add("dve", lambda e, col=col: e.tensor_tensor(
            dst_sb[:, col:col + 512], sb_rt[0][:, :], sb_rt[1][:, :], AL.add),
            deps=[t1, t2], name="rpR")
        B("rtb0").rd(tro)
        B("rtb1").rd(tro)
        B("cosb").rd(tro)
        return tro, tro

    q_fin, k_fin, v_fin = [], [], []
    for th in range(2):
        t_cos = dma(c_cos2[:, th * SM:(th + 1) * SM], sb_cosh[:, :],
                    deps=B("cosb").wdeps(), name="cosld", sem_key="ldcos")
        t_sin = dma(c_sinf[:, th * SM:(th + 1) * SM], sb_sinh[:, :],
                    deps=B("cosb").wdeps(), name="sinld", sem_key="ldcos")
        B("cosb").setw(t_sin)
        if STOP == "ln_cos":
            break
        xn_evs = []
        for ts in range(TSUB):
            xn_evs += emit_ln_transpose(x_b, th * SM + ts * 128, ts * 128)
        B("xnf").setw(xn_evs[-1])
        xn_rdy = xn_evs[-1]
        if STOP and STOP.startswith("ln"):
            break

        for ocg in range(NH // 2):
            oc0 = ocg * 2
            mm_stop = {}
            for dc in range(DC):
                st = dc % 4
                t_cv = stream_chunk(
                    wq_de[dc * 128:(dc + 1) * 128, oc0 * 128:(oc0 + 2) * 128],
                    256, st, "wq")
                for tt in range(NTT):
                    for oi in range(2):
                        k = oi * NTT + tt
                        deps = [t_cv, xn_rdy]
                        if dc == 0:
                            deps += ARN[k].wdeps()
                        t_mm = Sc.add("pe", lambda e, k=k, st=st, oi=oi, tt=tt, dc=dc:
                                      e.matmul(
                            PSA[:, k * 512:(k + 1) * 512],
                            sb_wbf[st][:, oi * 128:(oi + 1) * 128],
                            xn_col(dc, tt * 512),
                            start=(dc == 0), stop=(dc == DC - 1)),
                            deps=deps, name="qmm")
                        if dc == 0:
                            ARN[k].setw(t_mm)
                        B(f"wbfb{st}").rd(t_mm)
                        if dc == DC - 1:
                            mm_stop[k] = t_mm
                            B("xnf").rd(t_mm)
            for oi in range(2):
                for tt in range(NTT):
                    k = oi * NTT + tt
                    col = (oc0 + oi) * S + th * SM + tt * 512
                    tro, tri_ = rope_evac(mm_stop[k], k, col, th, tt, sb_Q, t_sin)
                    q_fin = [tro, tri_]

        if STOP == "q":
            break
        for tt in range(NTT):
            mm_stop = {}
            for dc in range(DC):
                st = dc % 4
                t_cv = stream_chunk(wk_de[dc * 128:(dc + 1) * 128, :], MYKV, st, "wk")
                for oc in range(NKV):
                    deps = [t_cv, xn_rdy]
                    if dc == 0:
                        deps += ARN[oc].wdeps()
                    t_mm = Sc.add("pe", lambda e, oc=oc, st=st, tt=tt, dc=dc: e.matmul(
                        PSA[:, oc * 512:(oc + 1) * 512],
                        sb_wbf[st][:, oc * 128:(oc + 1) * 128],
                        xn_col(dc, tt * 512),
                        start=(dc == 0), stop=(dc == DC - 1)),
                        deps=deps, name="kmm")
                    if dc == 0:
                        ARN[oc].setw(t_mm)
                    B(f"wbfb{st}").rd(t_mm)
                    if dc == DC - 1:
                        mm_stop[oc] = t_mm
                        B("xnf").rd(t_mm)
            for oc in range(NKV):
                col = oc * S + th * SM + tt * 512
                tro, tri_ = rope_evac(mm_stop[oc], oc, col, th, tt, sb_K, t_sin)
                k_fin = [tro, tri_]

        if STOP == "k":
            break
        for tqg in range(TSUB // 4):
            mm_stop = {}
            for dc in range(DC):
                st = dc % 4
                t_cv = stream_chunk(wv_sl[dc * 128:(dc + 1) * 128, :], MYKV, st, "wv")
                for tq in range(4):
                    deps = [t_cv, xn_rdy]
                    if dc == 0:
                        deps += ARN[tq].wdeps()
                    t_mm = Sc.add("pe", lambda e, tq=tq, tqg=tqg, st=st, dc=dc:
                                  e.matmul(
                        PSA[:, tq * 512:tq * 512 + MYKV],
                        sb_xn[:, dc * SM + (tqg * 4 + tq) * 128:
                              dc * SM + (tqg * 4 + tq + 1) * 128],
                        sb_wbf[st][:, 0:MYKV],
                        start=(dc == 0), stop=(dc == DC - 1)),
                        deps=deps, name="vmm")
                    if dc == 0:
                        ARN[tq].setw(t_mm)
                    B(f"wbfb{st}").rd(t_mm)
                    if dc == DC - 1:
                        mm_stop[tq] = t_mm
                        B("xnf").rd(t_mm)
            for tq in range(4):
                tg = th * TSUB + tqg * 4 + tq
                t_ev = Sc.add("act", lambda e, tq=tq, tg=tg: e.activation(
                    sb_V[:, tg * MYKV:(tg + 1) * MYKV],
                    PSA[:, tq * 512:tq * 512 + MYKV], AF.Copy, bias=0.0, scale=1.0),
                    deps=[mm_stop[tq]], name="vev")
                ARN[tq].rd(t_ev)
                v_fin = [t_ev]

    if STOP and (STOP in ("p1", "q", "k") or STOP.startswith("ln")):
        Sc.barrier("fin")
        return _emit()
    Sc.barrier("b3a")
    xn_evs = []
    for ts in range(TSUB):
        xn_evs += emit_ln_transpose(x_mine, ts * 128, ts * 128)
    B("xnf").setw(xn_evs[-1])
    xn_my = xn_evs[-1]

    Sc.barrier("b3b")
    t_rw = None
    for dc in range(DC):
        t_ld = dma(rWT[dc * 128:(dc + 1) * 128, :], sb_sst[:, 0:NE],
                   deps=B("sstb").wdeps(), name="rwld", sem_key="lds")
        B("sstb").setw(t_ld)
        t_rw = Sc.add("pool", lambda e, dc=dc: e.tensor_copy(
            sb_rwbf[:, dc * NE:(dc + 1) * NE], sb_sst[:, 0:NE]),
            deps=[t_ld], name="rwcv")
        B("sstb").rd(t_rw)
    t_bbf = None
    bw_ = min(1024, D)
    for j in range(max(1, D // 1024)):
        t_ld = dma(bb[:, j * bw_:(j + 1) * bw_], sb_bst[:, 0:bw_],
                   deps=B("bstb").wdeps(), name="bbld", sem_key="ldb")
        B("bstb").setw(t_ld)
        t_cv = Sc.add("pool", lambda e, bw_=bw_: e.tensor_copy(
            sb_bb16[:, 0:bw_], sb_bst[:, 0:bw_]),
            deps=[t_ld] + B("bb16b").wdeps(), name="bbcv")
        B("bb16b").setw(t_cv)
        B("bstb").rd(t_cv)
        t_bbf = Sc.add("pool", lambda e, j=j, bw_=bw_: e.dma_start(
            out=bb_bf_d[:, j * bw_:(j + 1) * bw_], in_=sb_bb16[:, 0:bw_]),
            deps=[t_cv], dma=True, name="bbst", sem_key="pst")
        B("bb16b").rd(t_bbf)
    t_ld = dma(rb[:, :], sb_bst[:, 0:NE], deps=B("bstb").wdeps(), name="rbld",
               sem_key="ldb")
    B("bstb").setw(t_ld)
    t_rbf = Sc.add("pool", lambda e: e.tensor_copy(
        sb_rbbf[:, 0:NE], sb_bst[:, 0:NE]), deps=[t_ld], name="rbcv")
    B("bstb").rd(t_rbf)
    t_emf = Sc.add("dve", lambda e: e.tensor_copy(sb_emf[:, :], sb_em[:, :]),
                   deps=[t_const], name="emf")
    t_mr0 = Sc.add("dve", lambda e: e.tensor_scalar(
        sb_eidx[:, :], sb_eidx[:, :], sb_emf[:, :], None, AL.is_equal),
        deps=[t_emf, t_const], name="mrep0")
    t_mr = Sc.add("act", lambda e: e.activation(
        sb_mrep[:, :], sb_eidx[:, :], AF.Copy, bias=0.0, scale=2.0),
        deps=[t_mr0], name="mrep")
    t_br = None
    for j in range(max(1, D // CW)):
        t_ld = dma(B_r[:, j * CW:(j + 1) * CW], sb_sst[:, 0:CW],
                   deps=B("sstb").wdeps(), name="brld", sem_key="lds")
        B("sstb").setw(t_ld)
        t_br = Sc.add("pool", lambda e, j=j: e.tensor_copy(
            sb_Brbf[:, j * CW:(j + 1) * CW], sb_sst[:, 0:CW]),
            deps=[t_ld], name="brcv")
        B("sstb").rd(t_br)

    t_mT = None
    for ts in range(TSUB):
        t_mm = None
        for dc in range(DC):
            deps = [t_rw, xn_my] + (AVB[0].wdeps() if dc == 0 else [])
            t_mm = Sc.add("pe", lambda e, dc=dc, ts=ts: e.matmul(
                AV0[:, 0:NE],
                sb_xn[:, dc * SM + ts * 128: dc * SM + (ts + 1) * 128],
                sb_rwbf[:, dc * NE:(dc + 1) * NE],
                start=(dc == 0), stop=False), deps=deps, name="lgmm")
            if dc == 0:
                AVB[0].setw(t_mm)
        t_mm = Sc.add("pe", lambda e: e.matmul(
            AV0[:, 0:NE], sb_ones[0:1, 0:128], sb_rbbf[0:1, 0:NE],
            start=False, stop=True), deps=[t_mm, t_rbf, t_const], name="lgbias")
        t_cm = Sc.add("dve", lambda e: e.tensor_scalar(
            sb_lg[:, :], AV0[:, 0:NE], 0.0, None, AL.add, AL.max,
            accum_out=sb_pm[:, :]), deps=[t_mm] + B("lgb").wdeps(), name="lgcm")
        B("lgb").setw(t_cm)
        AVB[0].rd(t_cm)
        t_nm = Sc.add("dve", lambda e: e.tensor_scalar(
            sb_pm[:, :], sb_pm[:, :], -1.0, None, AL.mult), deps=[t_cm], name="lgnm")
        t_ex = Sc.add("act", lambda e: e.activation(
            sb_pr[:, :], sb_lg[:, :], AF.Exp, bias=sb_pm[:, :], scale=1.0,
            accum_out=sb_ls[:, :]), deps=[t_nm] + B("prb").wdeps(), name="lgexp")
        B("lgb").rd(t_ex)
        t_rc = Sc.add("dve", lambda e: e.reciprocal(sb_rr[:, :], sb_ls[:, :]),
                      deps=[t_ex], name="lgrc")
        t_p = Sc.add("dve", lambda e: e.tensor_scalar(
            sb_pr[:, :], sb_pr[:, :], sb_rr[:, :], None, AL.mult),
            deps=[t_rc, t_ex], name="lgp")
        B("prb").setw(t_p)
        t_m8 = Sc.add("dve", lambda e: e.max(sb_t8[:, :], sb_pr[:, :]),
                      deps=[t_p], name="max8")
        t_ms = Sc.add("dve", lambda e: e.tensor_scalar(
            sb_msk[:, :], sb_pr[:, :], sb_t8[:, 7:8], None, AL.is_ge),
            deps=[t_m8], name="thr")
        t_mp = Sc.add("dve", lambda e: e.tensor_tensor(
            sb_mpb[:, :], sb_pr[:, :], sb_msk[:, :], AL.mult),
            deps=[t_ms] + B("mpbb").wdeps(), name="maskp")
        B("mpbb").setw(t_mp)
        B("prb").rd(t_mp)
        t_tr = Sc.add("pe", lambda e: e.transpose(
            PT0[0:NE, 0:128], sb_mpb[:, :], sb_ident[:, :]),
            deps=[t_mp, t_const] + PTB[0].wdeps(), name="mpT")
        PTB[0].setw(t_tr)
        B("mpbb").rd(t_tr)
        t_mT = Sc.add("act", lambda e, ts=ts: e.activation(
            sb_mT[:, ts * 128:(ts + 1) * 128], PT0[0:NE, 0:128], AF.Copy,
            bias=0.0, scale=1.0), deps=[t_tr], name="mpTe")
        PTB[0].rd(t_mT)

    t_wr = None
    for j in range(NTT):
        bank = 2 + j
        t_mm = Sc.add("pe", lambda e, j=j, bank=bank: e.matmul(
            PSA[:, bank * 512:(bank + 1) * 512],
            sb_mrep[:, :], sb_mT[:, j * 512:(j + 1) * 512],
            start=True, stop=True),
            deps=[t_mT, t_mr] + ARN[bank].wdeps(), name="wrmm")
        ARN[bank].setw(t_mm)
        t_wr = Sc.add("act", lambda e, j=j, bank=bank: e.activation(
            sb_wsb[:, j * 512:(j + 1) * 512], PSA[:, bank * 512:(bank + 1) * 512],
            AF.Copy, bias=0.0, scale=1.0), deps=[t_mm], name="wrev")
        ARN[bank].rd(t_wr)

    t_hw = None
    for tg in range(NTT):
        av = AVb[tg % 2]
        avb = AVB[tg % 2]
        t_mm = None
        for dc in range(DC):
            st = dc % 2
            t_ld = dma(A_T[dc * 128:(dc + 1) * 128, :], sb_sst[:, 0:ER],
                       deps=B("sstb").wdeps(), name="ald", sem_key="lds")
            B("sstb").setw(t_ld)
            t_cv = Sc.add("pool", lambda e, st=st: e.tensor_copy(
                sb_abf[st][:, :], sb_sst[:, 0:ER]),
                deps=[t_ld] + B(f"abfb{st}").wdeps(), name="acv")
            B(f"abfb{st}").setw(t_cv)
            B("sstb").rd(t_cv)
            deps = [t_cv, xn_my] + (avb.wdeps() if dc == 0 else [])
            t_mm = Sc.add("pe", lambda e, dc=dc, tg=tg, st=st, av=av: e.matmul(
                av[:, 0:512], sb_abf[st][:, :], xn_col(dc, tg * 512),
                start=(dc == 0), stop=(dc == DC - 1)),
                deps=deps, name="hmm")
            if dc == 0:
                avb.setw(t_mm)
            B(f"abfb{st}").rd(t_mm)
            if dc == DC - 1:
                B("xnf").rd(t_mm)
        t_hw = Sc.add("dve", lambda e, tg=tg, av=av: e.tensor_tensor(
            sb_hw[:, tg * 512:(tg + 1) * 512], av[:, 0:512],
            sb_wsb[:, tg * 512:(tg + 1) * 512], AL.mult),
            deps=[t_mm, t_wr], name="hwmul")
        avb.rd(t_hw)

    if STOP == "p3b":
        Sc.barrier("fin")
        return _emit()
    Sc.barrier("b2")
    SSTB = [B("sstb0"), B("sstb1")]
    MPBF = [B("mpbf0"), B("mpbf1")]
    RRB = [B("rrb0"), B("rrb1")]
    SSBB = [B("ssbb0"), B("ssbb1")]
    PBF = [B("pbuf0"), B("pbuf1")]
    PTSH = [B("ptsh0"), B("ptsh1")]
    s_lasts = [None, None]

    def attn_back(st_):
        h, qt, pp, nkb, t_ex, t_rc, kv = st_
        t_pte = {}
        for bj in range((nkb + 3) // 4):
            pb = bj % 2
            n4 = min(4, nkb - bj * 4)
            tts = []
            for i in range(n4):
                kb = bj * 4 + i
                tts.append(Sc.add("pe", lambda e, kb=kb, pb=pb, i=i, pp=pp:
                           e.transpose(
                    PTb[pb][:, i * 128:(i + 1) * 128],
                    sb_P[pp][:, kb * 128:(kb + 1) * 128], sb_ident[:, :]),
                    deps=[t_ex] + (PTB[pb].wdeps() if i == 0 else []),
                    name="pT"))
            PTB[pb].setw(tts[-1])
            PBF[pp].rd(tts[-1])
            half = bj % 2
            t_ev = Sc.add("act", lambda e, pb=pb, n4=n4, half=half: e.activation(
                sb_PT[:, half * 512:half * 512 + n4 * 128], PTb[pb][:, 0:n4 * 128],
                AF.Copy, bias=0.0, scale=1.0),
                deps=[tts[-1]] + PTSH[half].wdeps(), name="pTe")
            PTSH[half].setw(t_ev)
            PTB[pb].rd(t_ev)
            t_pte[bj] = t_ev
        t_av = None
        for kb in range(nkb):
            deps = [t_pte[kb // 4]] + v_fin
            if kb == 0:
                deps += AVB[pp].wdeps()
            slot = kb % 8
            t_av = Sc.add("pe", lambda e, kb=kb, pp=pp, kv=kv, slot=slot,
                          nkb=nkb: e.matmul(
                AVb[pp][:, 0:128],
                sb_PT[:, slot * 128:(slot + 1) * 128],
                sb_V[:, kb * MYKV + kv * 128: kb * MYKV + (kv + 1) * 128],
                start=(kb == 0), stop=(kb == nkb - 1)),
                deps=deps, name="av")
            if kb == 0:
                AVB[pp].setw(t_av)
            PTSH[(kb // 4) % 2].rd(t_av)
        t_se = Sc.add("act", lambda e, pp=pp: e.activation(
            sb_ssb[pp][:, :], AVb[pp][:, 0:128], AF.Identity,
            bias=0.0, scale=_st2[pp][:, 6:7]),
            deps=[t_av, t_rc] + SSBB[pp].wdeps(), name="sev")
        SSBB[pp].setw(t_se)
        AVB[pp].rd(t_se)
        RRB[pp].rd(t_se)
        t_st = dma(sb_ssb[pp][:, :],
                   s_loc[qt * 128:(qt + 1) * 128, h * 128:(h + 1) * 128],
                   deps=[t_se], name="sst", sem_key=f"sst{pp}")
        SSBB[pp].rd(t_st)
        s_lasts[pp] = t_st

    pending = None
    for h in range(NH):
        kv = h // NREP
        for qt in range(QT):
            pp = qt % 2
            kext = (qt + 1) * 128
            nkb = qt + 1
            nb = (kext + 511) // 512
            t_s = {}
            for kc in range(nb):
                w = min(512, kext - kc * 512)
                deps = q_fin + k_fin + ARN[kc].wdeps()
                t_s[kc] = Sc.add("pe", lambda e, kc=kc, w=w, h=h, qt=qt, kv=kv:
                                 e.matmul(
                    PSA[:, kc * 512:kc * 512 + w],
                    sb_Q[:, h * S + qt * 128: h * S + (qt + 1) * 128],
                    sb_K[:, kv * S + kc * 512: kv * S + kc * 512 + w],
                    start=True, stop=True), deps=deps, name="smm")
                ARN[kc].setw(t_s[kc])
            dslc = qt * 128 // 512
            t_msk = Sc.add("dve", lambda e, qt=qt: e.tensor_tensor(
                PSA[:, qt * 128:(qt + 1) * 128], PSA[:, qt * 128:(qt + 1) * 128],
                sb_tri[:, :], AL.add), deps=[t_s[dslc], t_const], name="smask")
            ARN[dslc].setw(t_msk)
            t_cm = {}
            for kc in range(nb):
                w = min(512, kext - kc * 512)
                deps = [t_s[kc]] + SSTB[pp].wdeps() + \
                    (MPBF[pp].wdeps() if kc == 0 else [])
                if kc == dslc:
                    deps.append(t_msk)
                t_cm[kc] = Sc.add("dve", lambda e, kc=kc, w=w, pp=pp: e.tensor_scalar(
                    sb_Ssb[pp][:, kc * 512:kc * 512 + w],
                    PSA[:, kc * 512:kc * 512 + w],
                    0.0, None, AL.add, AL.max, accum_out=_st2[pp][:, kc:kc + 1]),
                    deps=deps, name="scm")
                ARN[kc].rd(t_cm[kc])
            SSTB[pp].setw(t_cm[nb - 1])
            MPBF[pp].setw(t_cm[nb - 1])
            t_m = Sc.add("dve", lambda e, pp=pp, nb=nb: e.reduce_max(
                _st2[pp][:, 4:5], _st2[pp][:, 0:nb], AX.X),
                deps=[t_cm[nb - 1]] + B(f"nmb{pp}").wdeps(), name="smax")
            t_nm = Sc.add("dve", lambda e, pp=pp: e.tensor_scalar(
                _st2[pp][:, 4:5], _st2[pp][:, 4:5], -ISQ, None, AL.mult),
                deps=[t_m], name="snm")
            B(f"nmb{pp}").setw(t_nm)
            MPBF[pp].rd(t_m)
            if pending is not None:
                attn_back(pending)
            t_ex = Sc.add("act", lambda e, pp=pp, kext=kext: e.activation(
                sb_P[pp][:, 0:kext], sb_Ssb[pp][:, 0:kext], AF.Exp,
                bias=_st2[pp][:, 4:5], scale=ISQ, accum_out=_st2[pp][:, 5:6]),
                deps=[t_nm] + PBF[pp].wdeps() + B(f"lb{pp}").wdeps(), name="sexp")
            PBF[pp].setw(t_ex)
            B(f"lb{pp}").setw(t_ex)
            SSTB[pp].rd(t_ex)
            B(f"nmb{pp}").rd(t_ex)
            t_rc = Sc.add("dve", lambda e, pp=pp: e.reciprocal(
                _st2[pp][:, 6:7], _st2[pp][:, 5:6]), deps=[t_ex] + RRB[pp].wdeps(),
                name="srec")
            RRB[pp].setw(t_rc)
            B(f"lb{pp}").rd(t_rc)
            pending = (h, qt, pp, nkb, t_ex, t_rc, kv)
    attn_back(pending)

    if STOP == "p2":
        Sc.barrier("fin")
        return _emit()
    groups = [[2 * i, 2 * i + 1] for i in range(NCORES // 2)]
    t_cc = Sc.add("pool", lambda e: e.collective_compute(
        "AllGather", AL.bypass, replica_groups=groups,
        ins=[s_loc[:, :].opt()], outs=[s_gath[:, :].opt()]),
        deps=s_lasts, name="cc", sem_key="cc")
    t_cc.needs = True

    if STOP == "cc":
        Sc.add("pool", lambda e: e.nop(), deps=[t_cc], name="ccw")
        Sc.barrier("fin")
        return _emit()
    Sc.barrier("b3c")
    r_off = nc.sync.alloc_register("soff_reg")
    r_cur = nc.sync.alloc_register("soff_cur")
    t_off = Sc.add("sync", lambda e: e.reg_load(r_off, sb_soff[0:1, 0:1]),
                   deps=[t_const], name="soffld")
    sall_ev = None
    for rk in range(2):
        for ts in range(TSUB):
            st = ts % 2
            bsx = B(f"sxb{st}")
            base = (rk * S + ts * 128) * MYQ

            def _sxld(e, base=base, st=st):
                e.reg_add(r_cur, r_off, base)
                return e.dma_start(
                    out=sb_sxst[st][:, :],
                    in_=bass.AP(s_gath, r_cur, [[MYQ, 128], [1, MYQ]]))

            t_ld = Sc.add("sync", _sxld, deps=[t_cc, t_off] + bsx.wdeps(),
                          dma=True, name="sxld", sem_key=f"lsx{st}")
            bsx.setw(t_ld)
            for dg in range((MYQ // 128) // 4):
                pb = dg % 2
                tts = []
                for i in range(4):
                    dl = dg * 4 + i
                    tts.append(Sc.add("pe", lambda e, dl=dl, pb=pb, i=i, st=st:
                                      e.transpose(
                        PTb[pb][:, i * 128:(i + 1) * 128],
                        sb_sxst[st][:, dl * 128:(dl + 1) * 128], sb_ident[:, :]),
                        deps=[t_ld] + (PTB[pb].wdeps() if i == 0 else []),
                        name="sxT"))
                PTB[pb].setw(tts[-1])
                bsx.rd(tts[-1])
                ap = sb_sall[:, :].rearrange("p (d t) -> p d t", t=SM)
                dc0 = rk * (MYQ // 128) + dg * 4
                dst = ap[:, dc0:dc0 + 4, ts * 128:(ts + 1) * 128]
                sall_ev = Sc.add("act", lambda e, pb=pb, dst=dst: e.activation(
                    dst, PTb[pb][:, :], AF.Copy, bias=0.0, scale=1.0),
                    deps=[tts[-1]], name="sxTe")
                PTB[pb].rd(sall_ev)
    t_sall = sall_ev

    WQT = [B(f"wqt{i}") for i in range(8)]
    t_y_lasts = [None, None]
    for og in range(OG):
        t_wq = {}
        for w_i, (wsrc, wdst_sl, nrow) in enumerate(
                ((wo, sb_wosl, DCA), (bWT, sb_bwsl, DC))):
            for q4 in range(4):
                dc0 = q4 * (nrow // 4)
                ndc = nrow // 4
                qi = (w_i * 4 + q4) % 2
                bq = WQT[w_i * 4 + q4]
                bst = B(f"wfb{qi}")
                src_ap = wsrc[dc0 * 128:(dc0 + ndc) * 128, og * OGW:(og + 1) * OGW]
                src_ap = src_ap.rearrange("(d p) c -> p d c", p=128)
                dstf = sb_wf[qi][:, 0:ndc * OGW].rearrange(
                    "p (d c) -> p d c", c=OGW)
                deps = bst.wdeps() + (list(s_lasts) if og == 0 else [])
                t_ld = dma(src_ap, dstf, deps=deps, name="wsl",
                           sem_key=f"lwsl{qi}")
                bst.setw(t_ld)
                if (w_i * 4 + q4) % 2 == 0:
                    t_cv = Sc.add("dve", lambda e, qi=qi, wdst_sl=wdst_sl, dc0=dc0,
                                  ndc=ndc: e.tensor_copy(
                        wdst_sl[:, dc0 * OGW:(dc0 + ndc) * OGW],
                        sb_wf[qi][:, 0:ndc * OGW]),
                        deps=[t_ld] + bq.wdeps(), name="wslcv")
                else:
                    t_cv = Sc.add("act", lambda e, qi=qi, wdst_sl=wdst_sl, dc0=dc0,
                                  ndc=ndc: e.activation(
                        wdst_sl[:, dc0 * OGW:(dc0 + ndc) * OGW],
                        sb_wf[qi][:, 0:ndc * OGW], AF.Copy, bias=0.0, scale=1.0),
                        deps=[t_ld] + bq.wdeps(), name="wslcv")
                bq.setw(t_cv)
                bst.rd(t_cv)
                t_wq[(w_i, q4)] = t_cv
        bbb = B(f"bbsb{og % 2}")
        t_ld = dma(bb_bf_d[:, og * OGW:(og + 1) * OGW],
                   sb_bbs[0:1, (og % 2) * OGW:(og % 2 + 1) * OGW],
                   deps=bbb.wdeps() + ([t_bbf] if og < 2 else []),
                   name="bbsld", sem_key="lwb")
        bbb.setw(t_ld)
        t_wq_last = t_ld
        t_cv_all = [t_wq[(0, 3)], t_wq[(1, 3)], t_ld]
        for ts in range(TSUB):
            par = ts % 2
            ps0 = par * 512
            pg0 = 1024 + par * 512
            t_ms = None
            t_mg = None
            deps0 = ARN[par].wdeps() + [t_sall]
            for dc in range(max(DCA, DC)):
                if dc < DCA:
                    q4 = dc // (DCA // 4)
                    t_ms = Sc.add("pe", lambda e, dc=dc, ts=ts, ps0=ps0: e.matmul(
                        PSA[:, ps0:ps0 + OGW],
                        sb_sall[:, dc * SM + ts * 128: dc * SM + (ts + 1) * 128],
                        sb_wosl[:, dc * OGW:(dc + 1) * OGW],
                        start=(dc == 0), stop=(dc == DCA - 1)),
                        deps=(deps0 if dc == 0 else []) + [t_wq[(0, q4)]],
                        name="omm")
                    if dc == 0:
                        ARN[par].setw(t_ms)
                    if ts == TSUB - 1 and dc % (DCA // 4) == (DCA // 4) - 1:
                        WQT[q4].rd(t_ms)
                if dc < DC:
                    q4 = dc // (DC // 4)
                    t_mg = Sc.add("pe", lambda e, dc=dc, ts=ts, pg0=pg0: e.matmul(
                        PSA[:, pg0:pg0 + OGW],
                        sb_xn[:, dc * SM + ts * 128: dc * SM + (ts + 1) * 128],
                        sb_bwsl[:, dc * OGW:(dc + 1) * OGW],
                        start=(dc == 0), stop=False),
                        deps=([xn_my] + ARN[2 + par].wdeps() if dc == 0 else [])
                        + [t_wq[(1, q4)]], name="gmm")
                    if dc == 0:
                        ARN[2 + par].setw(t_mg)
                    if ts == TSUB - 1 and dc % (DC // 4) == (DC // 4) - 1:
                        WQT[4 + q4].rd(t_mg)
            t_mg = Sc.add("pe", lambda e, og=og, pg0=pg0: e.matmul(
                PSA[:, pg0:pg0 + OGW], sb_ones[0:1, 0:128],
                sb_bbs[0:1, (og % 2) * OGW:(og % 2 + 1) * OGW],
                start=False, stop=False),
                deps=[t_mg, t_wq_last, t_const], name="gbias")
            B(f"bbsb{og % 2}").rd(t_mg)
            t_mg = Sc.add("pe", lambda e, og=og, ts=ts, pg0=pg0: e.matmul(
                PSA[:, pg0:pg0 + OGW], sb_hw[:, ts * 128:(ts + 1) * 128],
                sb_Brbf[:, og * OGW:(og + 1) * OGW], start=False, stop=True),
                deps=[t_mg, t_hw, t_br], name="glora")
            t_g = Sc.add("act", lambda e, par=par, pg0=pg0: e.activation(
                sb_g[par][:, :], PSA[:, pg0:pg0 + OGW], AF.Silu,
                bias=0.0, scale=1.0), deps=[t_mg] + B(f"gb{par}").wdeps(),
                name="silu")
            B(f"gb{par}").setw(t_g)
            ARN[2 + par].rd(t_g)
            t_xr = dma(x_mine[ts * 128:(ts + 1) * 128, og * OGW:(og + 1) * OGW],
                       sb_xr[par][:, :], deps=B(f"xrb{par}").wdeps(), name="xrld",
                       sem_key=f"lxr{par}")
            B(f"xrb{par}").setw(t_xr)
            t_sg = Sc.add("dve", lambda e, par=par, ps0=ps0: e.tensor_tensor(
                sb_sg[par][:, :], PSA[:, ps0:ps0 + OGW], sb_g[par][:, :], AL.mult),
                deps=[t_ms, t_g] + B(f"sgb{par}").wdeps(), name="sgmul")
            B(f"sgb{par}").setw(t_sg)
            B(f"gb{par}").rd(t_sg)
            ARN[par].rd(t_sg)
            t_o = Sc.add("dve", lambda e, par=par: e.tensor_tensor(
                sb_out[par][:, :], sb_sg[par][:, :], sb_xr[par][:, :], AL.add),
                deps=[t_sg, t_xr] + B(f"ob{par}").wdeps(), name="oadd")
            B(f"ob{par}").setw(t_o)
            B(f"sgb{par}").rd(t_o)
            B(f"xrb{par}").rd(t_o)
            t_y = dma(sb_out[par][:, :],
                      y[ts * 128:(ts + 1) * 128, og * OGW:(og + 1) * OGW],
                      deps=[t_o], name="yst", sem_key=f"syt{par}")
            B(f"ob{par}").rd(t_y)
            t_y_lasts[par] = t_y

    Sc.add("sync", lambda e: e.nop(), deps=t_y_lasts, name="final_sync")
    Sc.add("pool", lambda e: e.nop(), deps=t_y_lasts + [t_bbf, t_cc],
           name="final_pool")

    return _emit()


def _host_consts(cfg):
    S, NE, R, E = cfg["S"], cfg["NE"], cfg["R"], cfg["E"]
    bf = ml_dtypes.bfloat16
    ident = np.eye(128, dtype=np.float32).astype(bf)
    tri = np.triu(np.full((128, 128), NEG, dtype=np.float32), k=1)
    freqs = (1.0 / (10000.0 ** (np.arange(0, 128, 2, dtype=np.float32)
                                / np.float32(128.0)))).astype(np.float32)
    t = np.arange(S, dtype=np.float32)
    ang = np.outer(t, freqs).astype(np.float32)
    cos_fm = np.cos(ang).astype(np.float32).T
    sin_fm = np.sin(ang).astype(np.float32).T
    cos2 = np.ascontiguousarray(np.vstack([cos_fm, cos_fm]))
    sinf = np.ascontiguousarray(np.vstack([-sin_fm, sin_fm]))
    swap = np.zeros((128, 128), dtype=np.float32)
    swap[(np.arange(128) + 64) % 128, np.arange(128)] = 1.0
    swap = np.ascontiguousarray(swap.astype(bf))
    eidx = np.ascontiguousarray(
        np.tile((np.arange(E * R, dtype=np.float32) // R)[None, :], (NE, 1)))
    ones = np.ones((1, 512), dtype=bf)
    f32s = np.zeros((128, 16), dtype=np.float32)
    vals = [0.0, 1.0, 1e-5, 1.0 / math.sqrt(128.0), 2.0]
    f32s[:, :len(vals)] = np.asarray(vals, dtype=np.float32)[None, :]
    return dict(c_ident=ident, c_tri=tri, c_cos2=cos2, c_sinf=sinf,
                c_swap=swap, c_eidx=eidx, c_ones=ones, c_f32s=f32s)


def _deinterleave(w, nheads):
    d = w.shape[0]
    return np.ascontiguousarray(
        w.reshape(d, nheads, 64, 2).transpose(0, 1, 3, 2).reshape(d, nheads * 128))


def make_in_maps(cfg, L0_attn_x, L0_attn_wq, L0_attn_wk, L0_attn_wv, L0_attn_wo,
                 L0_ffn_base_W, L0_ffn_base_b, L0_ffn_router_W, L0_ffn_router_b,
                 L0_ffn_A, L0_ffn_B, L0_ffn_expert_map):
    S, D, NH, NKV, NCORES = cfg["S"], cfg["D"], cfg["NH"], cfg["NKV"], cfg["NCORES"]
    SM, MYQ, MYKV = S // 2, NH * 128, NKV * 128
    f = np.ascontiguousarray
    x = np.asarray(L0_attn_x, dtype=np.float32)
    wq_d = _deinterleave(np.asarray(L0_attn_wq, np.float32), 2 * NH)
    wk_d = _deinterleave(np.asarray(L0_attn_wk, np.float32), 2 * NKV)
    wv = np.asarray(L0_attn_wv, np.float32)
    wo_ = f(np.asarray(L0_attn_wo, np.float32))
    bWT = f(np.asarray(L0_ffn_base_W, np.float32).T)
    bb = f(np.asarray(L0_ffn_base_b, np.float32).reshape(1, D))
    rWT = f(np.asarray(L0_ffn_router_W, np.float32).T)
    rb = f(np.asarray(L0_ffn_router_b, np.float32).reshape(1, -1))
    A_T = f(np.asarray(L0_ffn_A, np.float32).reshape(128, D).T)
    B_rr = f(np.asarray(L0_ffn_B, np.float32).transpose(0, 2, 1).reshape(128, D))
    em_ = f(np.asarray(L0_ffn_expert_map, np.int32).reshape(-1, 1))
    consts = _host_consts(cfg)
    maps = []
    for c in range(NCORES):
        b, hg = c // 2, c % 2
        m = dict(consts)
        m["x_b"] = f(x[b])
        m["x_mine"] = f(x[b, hg * SM:(hg + 1) * SM])
        m["wq_de"] = f(wq_d[:, hg * MYQ:(hg + 1) * MYQ])
        m["wk_de"] = f(wk_d[:, hg * MYKV:(hg + 1) * MYKV])
        m["wv_sl"] = f(wv[:, hg * MYKV:(hg + 1) * MYKV])
        m["wo"] = wo_
        m["bWT"] = bWT
        m["bb"] = bb
        m["rWT"] = rWT
        m["rb"] = rb
        m["A_T"] = A_T
        m["B_r"] = B_rr
        m["em"] = em_
        m["c_soff"] = np.array([[hg * SM * MYQ]], dtype=np.int32)
        maps.append(m)
    return maps


_NC_CACHE = {}


def kernel(**inputs):
    cfg = make_cfg(full=True)
    if "full" not in _NC_CACHE:
        _NC_CACHE["full"] = build_nc(cfg)
    nc = _NC_CACHE["full"]
    in_maps = make_in_maps(cfg, **inputs)
    res = run_bass_kernel_spmd(nc, in_maps, core_ids=list(range(cfg["NCORES"])))
    S, D = cfg["S"], cfg["D"]
    SM = S // 2
    out = np.empty((4, S, D), dtype=np.float32)
    for c in range(cfg["NCORES"]):
        b, hg = c // 2, c % 2
        out[b, hg * SM:(hg + 1) * SM] = res.results[c]["y"]
    return out
